# version 33
# baseline (speedup 1.0000x reference)
import numpy as np
import ml_dtypes
import concourse.bass as bass
import concourse.mybir as mybir
import concourse.bass_utils as _bass_utils
from concourse.bass_utils import run_bass_kernel_spmd
from contextlib import ExitStack

if not getattr(_bass_utils, "_ldw_patch", False):
    _orig_run_command = _bass_utils.run_command

    def _run_command_ldw(cmd, *a, **kw):
        if isinstance(cmd, list):
            cmd = ["--enable-ldw-opt=true" if c == "--enable-ldw-opt=false" else c
                   for c in cmd]
        return _orig_run_command(cmd, *a, **kw)

    _bass_utils.run_command = _run_command_ldw
    _bass_utils._ldw_patch = True

F32 = mybir.dt.float32
BF16 = mybir.dt.bfloat16
AF = mybir.ActivationFunctionType
ALU = mybir.AluOpType
AX = mybir.AxisListType
BFNP = ml_dtypes.bfloat16

NCORES = 8
D = 1024
T = 2048
TE = 2176
EXT_OFF = 3
NC_CHUNK = 17
N_TCH = 16
LN_EPS = 1e-5

ENGS = ("sync", "tensor", "scalar", "vector", "gpsimd")


class Prog:

    def __init__(self, sems):
        self.q = {e: [] for e in ENGS}
        self.sems = sems
        self.cnt = {k: 0 for k in sems}

    def op(self, eng, fn, inc=None, n=1):
        if inc is None:
            self.q[eng].append(fn)
            return None
        self.cnt[inc] += n
        sem = self.sems[inc]
        self.q[eng].append(lambda e, fn=fn, sem=sem, n=n: fn(e).then_inc(sem, n))
        return self.cnt[inc]

    def wait(self, eng, name, val):
        sem = self.sems[name]
        self.q[eng].append(lambda e, sem=sem, val=val: e.wait_ge(sem, val))

    def now(self, name):
        return self.cnt[name]

    def flush(self, block):
        for eng in ENGS:
            q = self.q[eng]
            if q:
                def body(e, q=tuple(q)):
                    for fn in q:
                        fn(e)
                getattr(block, eng)(body)
                self.q[eng] = []


def bf(a):
    return np.asarray(a, np.float32).astype(BFNP)



def pack_lhsT(w_T, kdim, odim, oc_outer=True):
    K, OC = kdim // 128, odim // 128
    w = np.asarray(w_T).reshape(K, 128, OC, 128)
    if oc_outer:
        packed = w.transpose(1, 2, 0, 3).reshape(128, OC * K * 128)
    else:
        packed = w.transpose(1, 0, 2, 3).reshape(128, K * OC * 128)
    return np.ascontiguousarray(packed)


def make_xT_ext(x, c):
    b, q = divmod(c, 4)
    t0 = 2048 * q - 3
    buf = np.zeros((TE, D), np.float32)
    lo, hi = max(t0, 0), min(t0 + 2054, 8192)
    buf[lo - t0: hi - t0, :] = x[b, lo:hi, :]
    return np.ascontiguousarray(buf.T)


def make_masks(c):
    q = c % 4
    u = np.arange(128)[:, None]
    t = np.arange(128)[None, :]
    band = ((u - t >= 0) & (u - t <= 6)).astype(np.float32)
    mask_first = band.copy()
    if q == 0:
        mask_first[0:3, :] = 0.0
    e = np.arange(8)[:, None]
    ext = ((128 + e - t >= 0) & (128 + e - t <= 6)).astype(np.float32)
    ext[6:8, :] = 0.0
    mask_ext_last = ext.copy()
    if q == 3:
        mask_ext_last[3:6, :] = 0.0
    return bf(mask_first), bf(band), bf(ext), bf(mask_ext_last)




(C_EPS8, C_EPS5, C_INV1024, C_ZERO, C_ONE, C_NHALF, C_NONE, C_EIGHTH,
 C_NEIGHTH) = range(9)


def emit_ln_group(P, R, sc, rsb, x_dst, g, track, banks=(4, 0, 5, 6)):
    pb = R["pb"]
    FC = R["fconst"]
    fcv = lambda j, p0=0, p1=128: FC[p0:p1, j:j + 1]
    st_a, ones_row = R["st_a"], R["ones_row"]
    rbf, rsq, stat, statb = sc["rbf"], sc["rsq"], sc["stat"], sc["statb"]
    osb = sc["osb"]
    bS1, bS2, bB1, bB2 = banks
    store_dves, gstores = track["store_dves"], track["gstores"]
    dver = track["dver"]
    if track.get("st4a"):
        P.wait("tensor", "act", track["st4a"])
    if track.get("st0d"):
        P.wait("tensor", "dve", track["st0d"])
    pe_stats = []
    for o_ in range(8):
        sl = (o_ % 2) * 512
        if o_ >= 2:
            P.wait("vector", "pe", pe_stats[o_ - 2])
            P.wait("scalar", "pe", pe_stats[o_ - 2])
        P.op("vector", lambda e, o=rbf[:, sl:sl + 512],
             i=rsb[:, o_ * 512:(o_ + 1) * 512]:
             e.tensor_copy(o, i), "dve", 1)
        drb = P.now("dve")
        P.wait("scalar", "dve", dver)
        P.op("scalar", lambda e, o=rsq[:, sl:sl + 512],
             i=rsb[:, o_ * 512:(o_ + 1) * 512]:
             e.activation(o, i, AF.Square), "act", 1)
        arq = P.now("act")
        P.wait("tensor", "dve", drb)
        P.op("tensor", lambda e, p=pb[bS1][0:1, 0:512], l=st_a[:, 0:1],
             r=rbf[:, sl:sl + 512],
             s=(o_ == 0): e.matmul(p, l, r, start=s, stop=(o_ == 7)), None)
        P.wait("tensor", "act", arq)
        P.op("tensor", lambda e, p=pb[bS2][0:1, 0:512], l=st_a[:, 0:1],
             r=rsq[:, sl:sl + 512],
             s=(o_ == 0): e.matmul(p, l, r, start=s, stop=(o_ == 7)), "pe", 1)
        pe_stats.append(P.now("pe"))
    peS2 = P.now("pe")
    P.wait("scalar", "pe", peS2)
    P.op("scalar", lambda e, o=stat[:, 0:512], i=pb[bS1][0:1, 0:512],
         sc_=fcv(C_INV1024, 0, 1):
         e.activation(o, i, AF.Identity, scale=sc_), None)
    P.op("scalar", lambda e, o=stat[:, 512:1024], i=stat[:, 0:512]:
         e.activation(o, i, AF.Square), "act", 1)
    amsq = P.now("act")
    track["st4a"] = amsq
    P.wait("vector", "act", amsq)
    P.op("vector", lambda e, o=stat[:, 512:1024], i0=pb[bS2][0:1, 0:512],
         i1=stat[:, 512:1024], sc_=fcv(C_INV1024, 0, 1):
         e.scalar_tensor_tensor(o, i0, sc_, i1, ALU.mult, ALU.subtract),
         "dve", 1)
    dvar = P.now("dve")
    track["st0d"] = dvar
    P.wait("scalar", "dve", dvar)
    P.op("scalar", lambda e, o=stat[:, 512:1024], i=stat[:, 512:1024],
         b=fcv(C_EPS5, 0, 1):
         e.activation(o, i, AF.Ln, bias=b), None)
    P.op("scalar", lambda e, o=statb[:, 0:512], i=stat[:, 512:1024],
         sc_=fcv(C_NHALF, 0, 1):
         e.activation(o, i, AF.Exp, scale=sc_), None)
    P.op("scalar", lambda e, o=stat[:, 512:1024], i=stat[:, 512:1024],
         sc_=fcv(C_NHALF, 0, 1):
         e.activation(o, i, AF.Exp, scale=sc_), "act", 1)
    arstd = P.now("act")
    P.wait("vector", "act", arstd)
    P.op("vector", lambda e, o=statb[:, 512:1024], i0=stat[:, 0:512],
         i1=stat[:, 512:1024]: e.tensor_mul(o, i0, i1), "dve", 1)
    dmr = P.now("dve")
    P.wait("tensor", "dve", dmr)
    if track.get("bcast"):
        P.wait("tensor", "dve", track["bcast"])
    P.op("tensor", lambda e, p=pb[bB1][:, 0:512], l=ones_row[:], r=statb[:, 0:512]:
         e.matmul(p, l, r, start=True, stop=True), None)
    P.op("tensor", lambda e, p=pb[bB2][:, 0:512], l=ones_row[:], r=statb[:, 512:1024]:
         e.matmul(p, l, r, start=True, stop=True), "pe", 1)
    peB = P.now("pe")
    P.wait("vector", "pe", peB)
    for fc_ in range(8):
        j = len(store_dves)
        if j >= 2:
            P.wait("vector", "g", gstores[j - 2])
        osl = osb[:, (j % 2) * 512:(j % 2) * 512 + 512]
        P.op("vector", lambda e, o=osl, i0=rsb[:, fc_ * 512:(fc_ + 1) * 512],
             i1=pb[bB1][:, 0:512]: e.tensor_mul(o, i0, i1), None)
        P.op("vector", lambda e, o=osl, i0=osl, i1=pb[bB2][:, 0:512]:
             e.tensor_sub(o, i0, i1), "dve", 1)
        dx = P.now("dve")
        store_dves.append(dx)
        P.wait("gpsimd", "dve", dx)
        P.op("gpsimd", lambda e, o=x_dst[fc_ * 128:(fc_ + 1) * 128,
                                         g * 512: g * 512 + 512],
             i=osb[:, (j % 2) * 512:(j % 2) * 512 + 512]:
             e.dma_start(out=o, in_=i), "g", 16)
        gstores.append(P.now("g"))
    track["bcast"] = P.now("dve")
    track["dver"] = P.now("dve")



def plan_l0_attn(nc, P, block, R, x_src, x_dst, es_outer):
    pb = R["pb"]
    pbf = R["pbf"]
    mir = R["mir"]
    mt = lambda k: mir[:, k * TE:(k + 1) * TE]
    FC = R["fconst"]
    fcv = lambda j, p0=0, p1=128: FC[p0:p1, j:j + 1]

    with ExitStack() as es:
        ent = es.enter_context
        wqp = ent(nc.sbuf_tensor("a_wqp", [128, 2 * 4 * 1024], BF16))
        wvo = ent(nc.sbuf_tensor("a_wvo", [128, 8 * 1024], BF16))
        wv = wvo
        wo = wvo
        proj = ent(nc.sbuf_tensor("a_proj", [128, 64], BF16))
        bout = ent(nc.sbuf_tensor("a_bout", [128, 8], F32))
        mks = ent(nc.sbuf_tensor("a_mks", [128, 256], BF16))
        mke = ent(nc.sbuf_tensor("a_mke", [8, 256], BF16))
        vtok = ent(nc.sbuf_tensor("a_vtok", [128, NC_CHUNK * 1026], BF16))
        phi = ent(nc.sbuf_tensor("a_phi", [128, 2 * TE], BF16))
        qks = ent(nc.sbuf_tensor("a_qks", [128, 2 * 512], BF16))
        er = ent(nc.sbuf_tensor("a_er", [128, 2 * 512], F32))
        ssb = ent(nc.sbuf_tensor("a_ssb", [128, 2 * 128], BF16))
        sse = ent(nc.sbuf_tensor("a_sse", [8, 2 * 128], BF16))
        zsb = ent(nc.sbuf_tensor("a_zsb", [128, 4], F32))
        atn = ent(nc.sbuf_tensor("a_atn", [128, 2 * 128], BF16))
        mbf = ent(nc.sbuf_tensor("a_mbf", [128, 8 * T], BF16))
        rbf = ent(nc.sbuf_tensor("a_rbf", [128, 2 * 512], BF16))
        rsq = ent(nc.sbuf_tensor("a_rsq", [128, 2 * 512], BF16))
        xf32 = ent(nc.sbuf_tensor("a_xf32", [128, 2 * 8 * 512], F32))
        srow = ent(nc.sbuf_tensor("a_srow", [2, 512], F32))
        stat = ent(nc.sbuf_tensor("a_stat", [1, 2 * 512], F32))
        statb = ent(nc.sbuf_tensor("a_statb", [1, 2 * 512], BF16))
        osb = ent(nc.sbuf_tensor("a_osb", [128, 2 * 512], F32))

        vt = lambda c: vtok[:, c * 1026: c * 1026 + 1026]

        W = R["W"]
        for dst, src in [
            (wv[:], W["wqkv_v"]),
            (proj[:], W["proj"]), (bout[:], W["b_out"]),
            (mks[:, 0:128], W["mask_first"]), (mks[:, 128:256], W["mask_mid"]),
            (mke[:, 0:128], W["mask_ext_mid"]), (mke[:, 128:256], W["mask_ext_last"]),
        ]:
            P.op("sync", lambda e, o=dst, i=src: e.dma_start(out=o, in_=i), "ld", 16)
        ld0 = P.now("ld")

        for k in range(8):
            P.op("gpsimd", lambda e, o=mt(k), i=x_src[k * 128:(k + 1) * 128, :]:
                 e.dma_start(out=o, in_=i), "g", 16)
        gmir = P.now("g")

        for c in range(NC_CHUNK):
            P.op("vector", lambda e, o=vt(c)[:, 1024:1025]: e.memset(o, 1.0), "dve", 1)

        bank_free_act = {}
        P.wait("tensor", "ld", ld0)
        P.wait("tensor", "g", gmir)
        for c in range(NC_CHUNK):
            for half in range(2):
                bk = (2 * c + half) % 2
                if (c, half) in ():
                    pass
                if bank_free_act.get(bk):
                    P.wait("tensor", "act", bank_free_act[bk])
                for k in range(8):
                    P.op("tensor", lambda e, p=pb[bk][:, 0:512],
                         l=mt(k)[:, c * 128:(c + 1) * 128],
                         r=wv[:, k * 1024 + half * 512: k * 1024 + half * 512 + 512],
                         s=(k == 0), t=(k == 7):
                         e.matmul(p, l, r, start=s, stop=t),
                         "pe" if k == 7 else None, 1)
                pev = P.now("pe")
                P.wait("scalar", "pe", pev)
                bank_free_act[bk] = P.op(
                    "scalar", lambda e, o=vt(c)[:, half * 512: half * 512 + 512],
                    i=pb[bk][:, 0:512]: e.activation(o, i, AF.Identity), "act", 1)

        P.wait("sync", "pe", P.now("pe"))
        P.op("sync", lambda e, o=wo[:], i=W["wout"]: e.dma_start(out=o, in_=i), "ld", 16)
        ld_wo = P.now("ld")

        last_fin = 0
        wq_pe = {}
        for hp in range(4):
            if hp > 0:
                P.wait("tensor", "dve", P.now("dve"))
            wslot = (hp % 2) * 4096
            if wq_pe.get(hp % 2):
                P.wait("sync", "pe", wq_pe[hp % 2])
            for j, oc in enumerate((2 * hp, 2 * hp + 1, 8 + 2 * hp, 9 + 2 * hp)):
                P.op("sync", lambda e, o=wqp[:, wslot + j * 1024: wslot + j * 1024 + 1024],
                     i=W["wqkv_qk"][:, oc * 1024:(oc + 1) * 1024]:
                     e.dma_start(out=o, in_=i), "ld", 16)
            ldqk = P.now("ld")
            P.wait("tensor", "ld", ldqk)
            for src_i, ocg in ((0, (2 * hp, 2 * hp + 1)), (1, (8 + 2 * hp, 9 + 2 * hp))):
                phv = phi[:, src_i * TE:(src_i + 1) * TE]
                for sub in range(5):
                    w = 512 if sub < 4 else 128
                    c0 = sub * 512
                    for ocx in range(2):
                        bk = 2 + ocx
                        if bank_free_act.get(bk):
                            P.wait("tensor", "act", bank_free_act[bk])
                        oc = ocg[ocx]
                        j = src_i * 2 + ocx
                        for k in range(8):
                            P.op("tensor", lambda e, p=pb[bk][:, 0:w],
                                 l=wqp[:, wslot + j * 1024 + k * 128:
                                        wslot + j * 1024 + k * 128 + 128],
                                 r=mt(k)[:, c0:c0 + w],
                                 s=(k == 0), t=(k == 7):
                                 e.matmul(p, l, r, start=s, stop=t),
                                 "pe" if k == 7 else None, 1)
                        pev = P.now("pe")
                        wq_pe[hp % 2] = pev
                        P.wait("scalar", "pe", pev)
                        bank_free_act[bk] = P.op(
                            "scalar", lambda e, o=qks[:, ocx * 512: ocx * 512 + w],
                            i=pb[bk][:, 0:w]: e.activation(o, i, AF.Identity), "act", 1)
                    aq = P.now("act")
                    if bank_free_act.get(4):
                        P.wait("tensor", "dve", bank_free_act[4])
                    P.wait("tensor", "act", aq)
                    for ocx in range(2):
                        P.op("tensor", lambda e, p=pb[4][ocx * 64:(ocx + 1) * 64, 0:w],
                             l=proj[:], r=qks[:, ocx * 512: ocx * 512 + w]:
                             e.matmul(p, l, r, start=True, stop=True),
                             "pe" if ocx == 1 else None, 1)
                    pev = P.now("pe")
                    P.wait("scalar", "pe", pev)
                    P.wait("scalar", "dve", last_fin)
                    P.op("scalar", lambda e, o=er[:, 0:w], i=pb[4][:, 0:w]:
                         e.activation(o, i, AF.Exp), "act", 1)
                    aE = P.now("act")
                    P.wait("vector", "act", aE)
                    P.op("vector", lambda e, o=er[:, 512:512 + w], i=pb[4][:, 0:w]:
                         e.tensor_scalar_max(o, i, 0.0), "dve", 1)
                    dmx = P.now("dve")
                    P.wait("vector", "dve", dmx)
                    fin = P.op("vector", lambda e, o=phv[:, c0:c0 + w],
                               i0=er[:, 0:w], i1=er[:, 512:512 + w], fc2=fcv(C_ONE):
                               e.scalar_tensor_tensor(o, i0, fc2, i1, ALU.min, ALU.add),
                               "dve", 1)
                    last_fin = fin
                    bank_free_act[4] = fin

            qph = phi[:, 0:TE]
            kph = phi[:, TE:2 * TE]
            dve_phi = P.now("dve")

            atn_free = {}
            sb_free = {}
            tr_free = {}
            peT_prev = 0
            P.wait("tensor", "dve", dve_phi)
            for c in range(N_TCH):
                sbk = 2 + (c % 2)
                abk = 5 + (c % 2)
                if sb_free.get(sbk):
                    P.wait("tensor", "dve", sb_free[sbk])
                for hx in range(2):
                    pr = 64 * hx
                    P.op("tensor", lambda e, p=pb[sbk][:, hx * 128: hx * 128 + 128],
                         l=kph[pr:pr + 64, c * 128: c * 128 + 128],
                         r=qph[pr:pr + 64, EXT_OFF + c * 128: EXT_OFF + c * 128 + 128]:
                         e.matmul(p, l, r, start=True, stop=True), None)
                    P.op("tensor", lambda e, p=pb[sbk][0:6, 256 + hx * 128: 256 + hx * 128 + 128],
                         l=kph[pr:pr + 64, c * 128 + 128: c * 128 + 134],
                         r=qph[pr:pr + 64, EXT_OFF + c * 128: EXT_OFF + c * 128 + 128]:
                         e.matmul(p, l, r, start=True, stop=True),
                         "pe" if hx == 1 else None, 1)
                peS = P.now("pe")
                P.wait("vector", "pe", peS)
                mm = mks[:, 0:128] if c == 0 else mks[:, 128:256]
                me = mke[0:6, 128:256] if c == N_TCH - 1 else mke[0:6, 0:128]
                for hx in range(2):
                    P.op("vector", lambda e, o=ssb[:, hx * 128:(hx + 1) * 128],
                         i0=pb[sbk][:, hx * 128: hx * 128 + 128], i1=mm:
                         e.tensor_mul(o, i0, i1), None)
                    P.op("vector", lambda e, o=sse[0:6, hx * 128: hx * 128 + 128],
                         i0=pb[sbk][0:6, 256 + hx * 128: 256 + hx * 128 + 128], i1=me:
                         e.tensor_mul(o, i0, i1), "dve", 1)
                sb_free[sbk] = P.now("dve")
                dveS = P.now("dve")
                if atn_free.get(abk):
                    P.wait("tensor", atn_free[abk][0], atn_free[abk][1])
                P.wait("tensor", "dve", dveS)
                for hx in range(2):
                    h = 2 * hp + hx
                    P.op("tensor", lambda e, p=pb[abk][:, hx * 160: hx * 160 + 128],
                         l=ssb[:, hx * 128:(hx + 1) * 128],
                         r=vt(c)[:, h * 128: h * 128 + 128]:
                         e.matmul(p, l, r, start=True, stop=False), None)
                    P.op("tensor", lambda e, p=pb[abk][:, hx * 160: hx * 160 + 128],
                         l=sse[0:6, hx * 128: hx * 128 + 128],
                         r=vt(c + 1)[0:6, h * 128: h * 128 + 128]:
                         e.matmul(p, l, r, start=False, stop=True), None)
                    P.op("tensor", lambda e, p=pb[abk][:, hx * 160 + 128: hx * 160 + 129],
                         l=ssb[:, hx * 128:(hx + 1) * 128],
                         r=vt(c)[:, 1024:1025]:
                         e.matmul(p, l, r, start=True, stop=False), None)
                    P.op("tensor", lambda e, p=pb[abk][:, hx * 160 + 128: hx * 160 + 129],
                         l=sse[0:6, hx * 128: hx * 128 + 128],
                         r=vt(c + 1)[0:6, 1024:1025]:
                         e.matmul(p, l, r, start=False, stop=True),
                         "pe" if hx == 1 else None, 1)
                peA = P.now("pe")
                P.wait("vector", "pe", peA)
                for hx in range(2):
                    P.op("vector", lambda e, o=zsb[:, hx:hx + 1],
                         i=pb[abk][:, hx * 160 + 128: hx * 160 + 129]:
                         e.tensor_copy(o, i), "dve", 1)
                dz = P.now("dve")
                P.wait("scalar", "dve", dz)
                P.op("scalar", lambda e, o=zsb[:, 2:4], i=zsb[:, 0:2], b=fcv(C_EPS8):
                     e.activation(o, i, AF.Ln, bias=b), None)
                P.op("scalar", lambda e, o=zsb[:, 2:4], i=zsb[:, 2:4], sc=fcv(C_NONE):
                     e.activation(o, i, AF.Exp, scale=sc), "act", 1)
                az = P.now("act")
                P.wait("scalar", "pe", peT_prev)
                for hx in range(2):
                    P.op("scalar", lambda e, o=atn[:, hx * 128:(hx + 1) * 128],
                         i=pb[abk][:, hx * 160: hx * 160 + 128],
                         s=zsb[:, 2 + hx:3 + hx]:
                         e.activation(o, i, AF.Identity, scale=s), "act", 1)
                atn_free[abk] = ("act", P.now("act"))
                actA = P.now("act")
                if tr_free.get(c % 2):
                    P.wait("tensor", tr_free[c % 2][0], tr_free[c % 2][1])
                P.wait("tensor", "act", actA)
                for hx in range(2):
                    P.op("tensor", lambda e, p=pbf[:, (c % 2) * 256 + hx * 128:
                                                    (c % 2) * 256 + (hx + 1) * 128],
                         i=atn[:, hx * 128:(hx + 1) * 128], idn=R["ident"]:
                         e.transpose(p, i, idn), "pe" if hx == 1 else None, 1)
                peT = P.now("pe")
                P.wait("scalar", "pe", peT)
                peT_prev = peT
                for hx in range(2):
                    h = 2 * hp + hx
                    P.op("scalar", lambda e, o=mbf[:, h * T + c * 128: h * T + c * 128 + 128],
                         i=pbf[:, (c % 2) * 256 + hx * 128:(c % 2) * 256 + (hx + 1) * 128]:
                         e.copy(o, i), "act", 1)
                tr_free[c % 2] = ("act", P.now("act"))

        dve_attn_done = P.now("dve")

        st_a, st_b, ones_row = R["st_a"], R["st_b"], R["ones_row"]
        P.wait("tensor", "dve", dve_attn_done)
        P.wait("tensor", "act", P.now("act"))
        P.wait("tensor", "ld", ld_wo)
        sc_ln = {"rbf": rbf, "rsq": rsq, "stat": stat, "statb": statb,
                 "osb": osb}
        track = {"store_dves": [], "gstores": [], "dver": 0}
        dver_g = {}
        apply_dve = {}
        for g in range(5):
            if g < 4:
                sl = (g % 2) * 4096
                P.wait("sync", "dve", apply_dve.get(g - 2, 0))
                for fc in range(8):
                    P.op("sync", lambda e, o=xf32[:, sl + fc * 512: sl + (fc + 1) * 512],
                         i=x_src[fc * 128:(fc + 1) * 128,
                                 EXT_OFF + g * 512: EXT_OFF + g * 512 + 512]:
                         e.dma_start(out=o, in_=i), "ld", 16)
                ldx = P.now("ld")
                for o_ in range(8):
                    bk = 2 + (o_ % 2)
                    if bank_free_act.get(("wo", bk)):
                        P.wait("tensor", "dve", bank_free_act[("wo", bk)])
                    for h in range(8):
                        P.op("tensor", lambda e, p=pb[bk][:, 0:512],
                             l=wo[:, (o_ * 8 + h) * 128:(o_ * 8 + h) * 128 + 128],
                             r=mbf[:, h * T + g * 512: h * T + g * 512 + 512],
                             s=(h == 0), t=(h == 7):
                             e.matmul(p, l, r, start=s, stop=t),
                             "pe" if h == 7 else None, 1)
                    pev = P.now("pe")
                    P.wait("vector", "pe", pev)
                    if o_ == 0:
                        P.wait("vector", "ld", ldx)
                    bank_free_act[("wo", bk)] = P.op(
                        "vector", lambda e, o=xf32[:, sl + o_ * 512: sl + (o_ + 1) * 512],
                        i=pb[bk][:, 0:512], s=bout[:, o_:o_ + 1],
                        x=xf32[:, sl + o_ * 512: sl + (o_ + 1) * 512]:
                        e.scalar_tensor_tensor(o, i, s, x, ALU.add, ALU.add), "dve", 1)
                dver_g[g] = P.now("dve")
            if g >= 1:
                gg = g - 1
                track["dver"] = dver_g[gg]
                emit_ln_group(P, R, sc_ln, xf32[:, (gg % 2) * 4096:(gg % 2) * 4096 + 4096],
                              x_dst, gg, track)
                apply_dve[gg] = track["dver"]

        P.flush(block)





def plan_l0_ffn(nc, P, block, R, x_src, x_dst):
    pb = R["pb"]
    mir = R["mir"]
    mt = lambda k: mir[:, k * TE:(k + 1) * TE]
    W = R["W"]

    with ExitStack() as es:
        ent = es.enter_context
        Aa = ent(nc.sbuf_tensor("f_Aa", [128, 32 * 512], BF16))
        Ab = ent(nc.sbuf_tensor("f_Ab", [128, 32 * 512], BF16))
        Ba = ent(nc.sbuf_tensor("f_Ba", [128, 32 * 512], BF16))
        Bb = ent(nc.sbuf_tensor("f_Bb", [128, 32 * 512], BF16))
        w12 = ent(nc.sbuf_tensor("f_w12", [128, 2 * 2048], BF16))
        hw = ent(nc.sbuf_tensor("f_hw", [128, 2 * 4096], BF16))
        stat = ent(nc.sbuf_tensor("f_stat", [1, 2 * 512], F32))
        scl = ent(nc.sbuf_tensor("f_scl", [128, 136], F32))
        xr = ent(nc.sbuf_tensor("f_xr", [128, 8 * 512], F32))
        rbf = Bb[:, 0:1024]
        rsq = Bb[:, 1024:2048]
        statb = Bb[0:1, 4096:5120]
        osb = Bb[:, 5120:7168].bitcast(F32)
        sil = xr[:, 0:1024]

        P.wait("tensor", "dve", P.now("dve"))
        P.wait("tensor", "act", P.now("act"))
        P.wait("sync", "g", P.now("g"))
        P.op("sync", lambda e, o=scl[:], i=W["l0f_scales"]:
             e.dma_start(out=o, in_=i), "ld", 16)
        P.wait("gpsimd", "pe", P.now("pe"))
        for k in range(8):
            P.op("gpsimd", lambda e, o=mt(k)[:, EXT_OFF:EXT_OFF + T],
                 i=x_src[k * 128:(k + 1) * 128, :]:
                 e.dma_start(out=o, in_=i), "g", 16)
        gmir = P.now("g")
        ld0 = P.now("ld")
        P.wait("tensor", "g", gmir)
        P.wait("tensor", "ld", ld0)

        sc_ln = {"rbf": rbf, "rsq": rsq, "stat": stat, "statb": statb,
                 "osb": osb}
        track = {"store_dves": [], "gstores": [], "dver": 0}
        slab_pe = {}
        bankF = {}

        def bwait(eng, bk):
            if bankF.get(bk):
                P.wait(eng, bankF[bk][0], bankF[bk][1])

        for tp in range(2):
            cA = EXT_OFF + (2 * tp) * 512
            cB = cA + 512
            P.wait("tensor", "dve", P.now("dve"))
            P.wait("tensor", "act", P.now("act"))
            for oc in range(32):
                slot = oc % 2
                b0 = slot * 4
                if slab_pe.get(("w12", slot)):
                    P.wait("sync", "pe", slab_pe[("w12", slot)])
                P.op("sync", lambda e, o=w12[:, slot * 2048: slot * 2048 + 1024],
                     i=W["w1p"][:, oc * 1024:(oc + 1) * 1024]:
                     e.dma_start(out=o, in_=i), "ld", 16)
                P.op("sync", lambda e, o=w12[:, slot * 2048 + 1024: slot * 2048 + 2048],
                     i=W["w2p"][:, oc * 1024:(oc + 1) * 1024]:
                     e.dma_start(out=o, in_=i), "ld", 16)
                ldw = P.now("ld")
                P.wait("tensor", "ld", ldw)
                for bk in range(b0, b0 + 4):
                    bwait("tensor", bk)
                for k in range(8):
                    for i2, cc in ((0, cA), (1, cB)):
                        P.op("tensor", lambda e, p=pb[b0 + i2][:, 0:512],
                             l=w12[:, slot * 2048 + k * 128: slot * 2048 + k * 128 + 128],
                             r=mt(k)[:, cc:cc + 512],
                             s=(k == 0), st=(k == 7):
                             e.matmul(p, l, r, start=s, stop=st),
                             "pe" if (k == 7 and i2 == 1) else None, 1)
                pe1 = P.now("pe")
                for k in range(8):
                    for i2, cc in ((0, cA), (1, cB)):
                        P.op("tensor", lambda e, p=pb[b0 + 2 + i2][:, 0:512],
                             l=w12[:, slot * 2048 + 1024 + k * 128:
                                    slot * 2048 + 1024 + k * 128 + 128],
                             r=mt(k)[:, cc:cc + 512],
                             s=(k == 0), st=(k == 7):
                             e.matmul(p, l, r, start=s, stop=st),
                             "pe" if (k == 7 and i2 == 1) else None, 1)
                pe2 = P.now("pe")
                slab_pe[("w12", slot)] = pe2
                P.wait("scalar", "pe", pe1)
                P.wait("scalar", "dve", slab_pe.get(("sil",), 0))
                for i2 in range(2):
                    aS = P.op("scalar", lambda e,
                              o=sil[:, i2 * 512: i2 * 512 + 512],
                              i=pb[b0 + i2][:, 0:512], sc_=scl[:, oc:oc + 1]:
                              e.activation(o, i, AF.Silu, scale=sc_), "act", 1)
                    bankF[b0 + i2] = ("act", aS)
                aS2 = P.now("act")
                P.wait("vector", "pe", pe2)
                P.wait("vector", "act", aS2)
                for i2, dst in ((0, Aa), (1, Ab)):
                    dA = P.op("vector", lambda e, o=dst[:, oc * 512:(oc + 1) * 512],
                              i0=pb[b0 + 2 + i2][:, 0:512], s_=scl[:, 32 + oc:33 + oc],
                              i1=sil[:, i2 * 512: i2 * 512 + 512]:
                              e.scalar_tensor_tensor(o, i0, s_, i1, ALU.mult, ALU.mult),
                              "dve", 1)
                    bankF[b0 + 2 + i2] = ("dve", dA)
                slab_pe[("sil",)] = P.now("dve")
            dve_A = P.now("dve")
            P.wait("tensor", "dve", dve_A)
            for oc in range(32):
                slot = oc % 2
                b0 = slot * 2
                if slab_pe.get(("hw", slot)):
                    P.wait("sync", "pe", slab_pe[("hw", slot)])
                P.op("sync", lambda e, o=hw[:, slot * 4096:(slot + 1) * 4096],
                     i=W["h1p"][:, oc * 4096:(oc + 1) * 4096]:
                     e.dma_start(out=o, in_=i), "ld", 16)
                ldw = P.now("ld")
                P.wait("tensor", "ld", ldw)
                bwait("tensor", b0)
                bwait("tensor", b0 + 1)
                for k in range(32):
                    for i2, src in ((0, Aa), (1, Ab)):
                        P.op("tensor", lambda e, p=pb[b0 + i2][:, 0:512],
                             l=hw[:, slot * 4096 + k * 128: slot * 4096 + k * 128 + 128],
                             r=src[:, k * 512:(k + 1) * 512],
                             s=(k == 0), st=(k == 31):
                             e.matmul(p, l, r, start=s, stop=st),
                             "pe" if (k == 31 and i2 == 1) else None, 1)
                peh = P.now("pe")
                slab_pe[("hw", slot)] = peh
                P.wait("scalar", "pe", peh)
                for i2, dst in ((0, Ba), (1, Bb)):
                    aG = P.op("scalar", lambda e, o=dst[:, oc * 512:(oc + 1) * 512],
                              i=pb[b0 + i2][:, 0:512], sc_=scl[:, 64 + oc:65 + oc]:
                              e.activation(o, i, AF.Gelu, scale=sc_), "act", 1)
                    bankF[b0 + i2] = ("act", aG)
            act_B = P.now("act")
            P.wait("tensor", "act", act_B)
            for oc in range(32):
                slot = oc % 2
                b0 = 4 + slot * 2
                if slab_pe.get(("hw2", slot)):
                    P.wait("sync", "pe", slab_pe[("hw2", slot)])
                P.op("sync", lambda e, o=hw[:, slot * 4096:(slot + 1) * 4096],
                     i=W["h2p"][:, oc * 4096:(oc + 1) * 4096]:
                     e.dma_start(out=o, in_=i), "ld", 16)
                ldw = P.now("ld")
                P.wait("tensor", "ld", ldw)
                bwait("tensor", b0)
                bwait("tensor", b0 + 1)
                for k in range(32):
                    for i2, src in ((0, Ba), (1, Bb)):
                        P.op("tensor", lambda e, p=pb[b0 + i2][:, 0:512],
                             l=hw[:, slot * 4096 + k * 128: slot * 4096 + k * 128 + 128],
                             r=src[:, k * 512:(k + 1) * 512],
                             s=(k == 0), st=(k == 31):
                             e.matmul(p, l, r, start=s, stop=st),
                             "pe" if (k == 31 and i2 == 1) else None, 1)
                peh = P.now("pe")
                slab_pe[("hw2", slot)] = peh
                P.wait("scalar", "pe", peh)
                for i2, dst in ((0, Aa), (1, Ab)):
                    aG = P.op("scalar", lambda e, o=dst[:, oc * 512:(oc + 1) * 512],
                              i=pb[b0 + i2][:, 0:512], sc_=scl[:, 96 + oc:97 + oc]:
                              e.activation(o, i, AF.Gelu, scale=sc_), "act", 1)
                    bankF[b0 + i2] = ("act", aG)
            act_C = P.now("act")
            P.wait("tensor", "act", act_C)
            for i2, src in ((0, Aa), (1, Ab)):
                t = 2 * tp + i2
                P.wait("sync", "dve", track["dver"])
                for fc_ in range(8):
                    P.op("sync", lambda e, o=xr[:, fc_ * 512:(fc_ + 1) * 512],
                         i=x_src[fc_ * 128:(fc_ + 1) * 128, t * 512: t * 512 + 512]:
                         e.dma_start(out=o, in_=i), "ld", 16)
                ldx = P.now("ld")
                for o_ in range(8):
                    slot = o_ % 2
                    if slab_pe.get(("w3", slot, i2)):
                        P.wait("sync", "pe", slab_pe[("w3", slot, i2)])
                    P.op("sync", lambda e, o=hw[:, slot * 4096:(slot + 1) * 4096],
                         i=W["w3p"][:, o_ * 4096:(o_ + 1) * 4096]:
                         e.dma_start(out=o, in_=i), "ld", 16)
                    ldw = P.now("ld")
                    P.wait("tensor", "ld", ldw)
                    bwait("tensor", slot)
                    for k in range(32):
                        P.op("tensor", lambda e, p=pb[slot][:, 0:512],
                             l=hw[:, slot * 4096 + k * 128: slot * 4096 + k * 128 + 128],
                             r=src[:, k * 512:(k + 1) * 512],
                             s=(k == 0), st=(k == 31):
                             e.matmul(p, l, r, start=s, stop=st),
                             "pe" if k == 31 else None, 1)
                    pew = P.now("pe")
                    slab_pe[("w3", slot, i2)] = pew
                    P.wait("vector", "pe", pew)
                    if o_ == 0:
                        P.wait("vector", "ld", ldx)
                    dW = P.op("vector", lambda e, o=xr[:, o_ * 512:(o_ + 1) * 512],
                              i0=pb[slot][:, 0:512], s_=scl[:, 128 + o_:129 + o_],
                              x=xr[:, o_ * 512:(o_ + 1) * 512]:
                              e.scalar_tensor_tensor(o, i0, s_, x, ALU.mult, ALU.add),
                              "dve", 1)
                    bankF[slot] = ("dve", dW)
                track["dver"] = P.now("dve")
                emit_ln_group(P, R, sc_ln, xr, x_dst, t, track)
                P.wait("tensor", "dve", P.now("dve"))
                P.wait("tensor", "act", P.now("act"))

        P.flush(block)


def pack_l0_ffn(inp):
    d = {}
    d["w1p"] = pack_lhsT(bf(np.asarray(inp["L0_ffn_w1_w"], np.float32).T), 1024, 4096)
    d["w2p"] = pack_lhsT(bf(np.asarray(inp["L0_ffn_w2_w"], np.float32).T), 1024, 4096)
    d["h1p"] = pack_lhsT(bf(np.asarray(inp["L0_ffn_h1_w"], np.float32).T), 4096, 4096)
    d["h2p"] = pack_lhsT(bf(np.asarray(inp["L0_ffn_h2_w"], np.float32).T), 4096, 4096)
    d["w3p"] = pack_lhsT(bf(np.asarray(inp["L0_ffn_w3_w"], np.float32).T), 4096, 1024)
    scl = np.zeros((128, 136), np.float32)
    scl[:, 0:32] = np.asarray(inp["L0_ffn_w1_s"], np.float32).reshape(32, 128).T
    scl[:, 32:64] = np.asarray(inp["L0_ffn_w2_s"], np.float32).reshape(32, 128).T
    scl[:, 64:96] = np.asarray(inp["L0_ffn_h1_s"], np.float32).reshape(32, 128).T
    scl[:, 96:128] = np.asarray(inp["L0_ffn_h2_s"], np.float32).reshape(32, 128).T
    scl[:, 128:136] = np.asarray(inp["L0_ffn_w3_s"], np.float32).reshape(8, 128).T
    d["l0f_scales"] = np.ascontiguousarray(scl)
    return d


L0F_DRAM = [
    ("w1p", [128, 32 * 1024], BF16), ("w2p", [128, 32 * 1024], BF16),
    ("h1p", [128, 32 * 4096], BF16), ("h2p", [128, 32 * 4096], BF16),
    ("w3p", [128, 8 * 4096], BF16), ("l0f_scales", [128, 136], F32),
]





def plan_l1_attn(nc, P, block, R, x_src, x_dst, z1cc_in, z1cc_out):
    pb = R["pb"]
    mir = R["mir"]
    mt = lambda k: mir[:, k * TE:(k + 1) * TE]
    W = R["W"]
    FC = R["fconst"]
    fcv = lambda j, p0=0, p1=128: FC[p0:p1, j:j + 1]

    with ExitStack() as es:
        ent = es.enter_context
        wbuf = ent(nc.sbuf_tensor("l_wbuf", [128, 2 * 8192], BF16))
        arena = ent(nc.sbuf_tensor("l_arena", [128, 32 * 1024], BF16))
        z1f = ent(nc.sbuf_tensor("l_z1f", [128, 1024], F32))
        z2f = ent(nc.sbuf_tensor("l_z2f", [128, 1024], F32))
        z2b = ent(nc.sbuf_tensor("l_z2b", [128, 8 * 128], BF16))
        mx = ent(nc.sbuf_tensor("l_mx", [128, 32], F32))
        den = ent(nc.sbuf_tensor("l_den", [128, 32], F32))
        xr = ent(nc.sbuf_tensor("l_xr", [128, 2 * 8 * 512], F32))
        rbf = ent(nc.sbuf_tensor("l_rbf", [128, 2 * 512], BF16))
        rsq = ent(nc.sbuf_tensor("l_rsq", [128, 2 * 512], BF16))
        stat = ent(nc.sbuf_tensor("l_stat", [1, 2 * 512], F32))
        statb = ent(nc.sbuf_tensor("l_statb", [1, 2 * 512], BF16))
        osb = ent(nc.sbuf_tensor("l_osb", [128, 2 * 512], F32))

        qtok = lambda c: arena[:, c * 1024:(c + 1) * 1024]
        ktok = lambda c: arena[:, 16384 + c * 1024: 16384 + (c + 1) * 1024]
        vsb = lambda oc: arena[:, oc * 2048:(oc + 1) * 2048]
        zm = lambda zc: arena[:, 16384 + zc * 2048: 16384 + (zc + 1) * 2048]

        P.wait("tensor", "dve", P.now("dve"))
        P.wait("tensor", "act", P.now("act"))
        P.wait("sync", "g", P.now("g"))
        P.op("sync", lambda e, o=wbuf[:, 0:8192], i=W["wqT"]:
             e.dma_start(out=o, in_=i), "ld", 16)
        P.op("sync", lambda e, o=wbuf[:, 8192:16384], i=W["wkT"]:
             e.dma_start(out=o, in_=i), "ld", 16)
        ld0 = P.now("ld")
        P.wait("gpsimd", "pe", P.now("pe"))
        for k in range(8):
            P.op("gpsimd", lambda e, o=mt(k)[:, EXT_OFF:EXT_OFF + T],
                 i=x_src[k * 128:(k + 1) * 128, :]:
                 e.dma_start(out=o, in_=i), "g", 16)
        gmir = P.now("g")
        P.wait("tensor", "g", gmir)
        P.wait("tensor", "ld", ld0)

        bankF = {}
        def bwait(eng, bk):
            if bankF.get(bk):
                P.wait(eng, bankF[bk][0], bankF[bk][1])
        for c in range(N_TCH):
            for qk, base, dst in ((0, 0, qtok(c)), (1, 8192, ktok(c))):
                for half in range(2):
                    bk = qk * 2 + half
                    bwait("tensor", bk)
                    for k in range(8):
                        P.op("tensor", lambda e, p=pb[bk][:, 0:512],
                             l=mt(k)[:, EXT_OFF + c * 128: EXT_OFF + c * 128 + 128],
                             r=wbuf[:, base + k * 1024 + half * 512:
                                    base + k * 1024 + half * 512 + 512],
                             s=(k == 0), st=(k == 7):
                             e.matmul(p, l, r, start=s, stop=st),
                             "pe" if k == 7 else None, 1)
                    pev = P.now("pe")
                    P.wait("scalar", "pe", pev)
                    aE = P.op("scalar", lambda e,
                              o=dst[:, half * 512: half * 512 + 512],
                              i=pb[bk][:, 0:512]:
                              e.activation(o, i, AF.Identity), "act", 1)
                    bankF[bk] = ("act", aE)
        act_qk = P.now("act")
        pe_qk = P.now("pe")

        P.wait("tensor", "act", act_qk)
        for hp in range(8):
            bk = 4 + hp // 4
            col = (hp % 4) * 128
            for c in range(N_TCH):
                P.op("tensor", lambda e, p=pb[bk][:, col:col + 128],
                     l=ktok(c)[:, hp * 128: hp * 128 + 128],
                     r=qtok(c)[:, hp * 128: hp * 128 + 128],
                     s=(c == 0), st=(c == N_TCH - 1):
                     e.matmul(p, l, r, start=s, stop=st),
                     "pe" if c == N_TCH - 1 else None, 1)
        pez1 = P.now("pe")
        P.wait("vector", "pe", pez1)
        P.op("vector", lambda e, o=z1f[:, 0:512], i=pb[4][:, 0:512]:
             e.tensor_copy(o, i), None)
        P.op("vector", lambda e, o=z1f[:, 512:1024], i=pb[5][:, 0:512]:
             e.tensor_copy(o, i), "dve", 1)
        dz1 = P.now("dve")
        P.wait("gpsimd", "dve", dz1)
        P.op("gpsimd", lambda e, o=z1cc_in[:], i=z1f[:]:
             e.dma_start(out=o, in_=i), "g", 16)
        P.op("gpsimd", lambda e: e.collective_compute(
            "AllReduce", ALU.add,
            replica_groups=[[0, 1, 2, 3], [4, 5, 6, 7]],
            ins=[z1cc_in.opt()], outs=[z1cc_out.opt()]), "pool", 1)
        ccn = P.now("pool")
        P.wait("gpsimd", "pool", ccn)
        P.op("gpsimd", lambda e, o=z1f[:], i=z1cc_out[:]:
             e.dma_start(out=o, in_=i), "g", 16)
        gz1 = P.now("g")

        P.wait("sync", "pe", pe_qk)
        P.op("sync", lambda e, o=wbuf[:, 0:8192], i=W["wvp"]:
             e.dma_start(out=o, in_=i), "ld", 16)
        P.op("sync", lambda e, o=wbuf[:, 8192:16384], i=W["wop"]:
             e.dma_start(out=o, in_=i), "ld", 16)
        ldv = P.now("ld")
        P.wait("tensor", "ld", ldv)
        for oc in range(8):
            for tc2 in range(4):
                bk = (oc * 4 + tc2) % 4
                bwait("tensor", bk)
                for k in range(8):
                    P.op("tensor", lambda e, p=pb[bk][:, 0:512],
                         l=wbuf[:, (oc * 8 + k) * 128:(oc * 8 + k) * 128 + 128],
                         r=mt(k)[:, EXT_OFF + tc2 * 512: EXT_OFF + tc2 * 512 + 512],
                         s=(k == 0), st=(k == 7):
                         e.matmul(p, l, r, start=s, stop=st),
                         "pe" if k == 7 else None, 1)
                pev = P.now("pe")
                P.wait("scalar", "pe", pev)
                aE = P.op("scalar", lambda e,
                          o=vsb(oc)[:, tc2 * 512: tc2 * 512 + 512],
                          i=pb[bk][:, 0:512]:
                          e.activation(o, i, AF.Identity), "act", 1)
                bankF[bk] = ("act", aE)

        P.wait("vector", "g", gz1)
        P.wait("scalar", "g", gz1)
        P.op("vector", lambda e, o=den[:, 0:8]: e.memset(o, 0.0), None)
        P.op("vector", lambda e, o=z2b[:]: e.memset(o, 0.0), "dve", 1)
        for h in range(16):
            hp, par = h // 2, h % 2
            p0, p1 = par * 64, par * 64 + 64
            cb = hp * 128 + par * 64
            sl = z1f[p0:p1, cb:cb + 64]
            P.op("vector", lambda e, o=mx[p0:p1, h:h + 1], i=sl:
                 e.reduce_max(o, i, axis=AX.X), None)
            P.op("vector", lambda e, o=mx[p0:p1, 16 + h:17 + h],
                 i=mx[p0:p1, h:h + 1], s_=fcv(C_NEIGHTH, p0, p1):
                 e.tensor_scalar_mul(o, i, s_), "dve", 1)
            dm = P.now("dve")
            P.wait("scalar", "dve", dm)
            P.op("scalar", lambda e, o=z2f[p0:p1, cb:cb + 64], i=sl,
                 sc_=fcv(C_EIGHTH, p0, p1), b=mx[p0:p1, 16 + h:17 + h],
                 acc=den[p0:p1, hp:hp + 1]:
                 e.activation(o, i, AF.Exp, scale=sc_, bias=b, accum_out=acc),
                 "act", 1)
        aexp = P.now("act")
        P.wait("scalar", "act", aexp)
        P.op("scalar", lambda e, o=den[:, 16:24], i=den[:, 0:8]:
             e.activation(o, i, AF.Ln), None)
        P.op("scalar", lambda e, o=den[:, 16:24], i=den[:, 16:24],
             sc_=fcv(C_NONE):
             e.activation(o, i, AF.Exp, scale=sc_), "act", 1)
        azr = P.now("act")
        P.wait("vector", "act", azr)
        for h in range(16):
            hp, par = h // 2, h % 2
            p0, p1 = par * 64, par * 64 + 64
            cb = hp * 128 + par * 64
            P.op("vector", lambda e, o=z2b[p0:p1, cb:cb + 64],
                 i=z2f[p0:p1, cb:cb + 64], s_=den[p0:p1, 16 + hp:17 + hp]:
                 e.tensor_scalar_mul(o, i, s_), "dve", 1)
        dz2 = P.now("dve")

        P.wait("tensor", "dve", dz2)
        for hp in range(8):
            for tc2 in range(4):
                i_ = hp * 4 + tc2
                bk = 4 + i_ % 3
                bwait("tensor", bk)
                P.op("tensor", lambda e, p=pb[bk][:, 0:512],
                     l=z2b[:, hp * 128: hp * 128 + 128],
                     r=vsb(hp)[:, tc2 * 512: tc2 * 512 + 512]:
                     e.matmul(p, l, r, start=True, stop=True), "pe", 1)
                pev = P.now("pe")
                if i_ % 2:
                    P.wait("vector", "pe", pev)
                    dE = P.op("vector", lambda e,
                              o=zm(hp)[:, tc2 * 512: tc2 * 512 + 512],
                              i=pb[bk][:, 0:512]:
                              e.tensor_copy(o, i), "dve", 1)
                    bankF[bk] = ("dve", dE)
                    continue
                P.wait("scalar", "pe", pev)
                aE = P.op("scalar", lambda e,
                          o=zm(hp)[:, tc2 * 512: tc2 * 512 + 512],
                          i=pb[bk][:, 0:512]:
                          e.activation(o, i, AF.Identity), "act", 1)
                bankF[bk] = ("act", aE)
        act_zm = P.now("act")
        dve_zm = P.now("dve")

        sc_ln = {"rbf": rbf, "rsq": rsq, "stat": stat, "statb": statb,
                 "osb": osb}
        track = {"store_dves": [], "gstores": [], "dver": 0}
        P.wait("tensor", "act", act_zm)
        P.wait("tensor", "dve", dve_zm)
        dver_g = {}
        apply_dve = {}
        for g in range(5):
            if g < 4:
                sl = (g % 2) * 4096
                P.wait("sync", "dve", apply_dve.get(g - 2, 0))
                for fc_ in range(8):
                    P.op("sync", lambda e, o=xr[:, sl + fc_ * 512: sl + (fc_ + 1) * 512],
                         i=x_src[fc_ * 128:(fc_ + 1) * 128, g * 512: g * 512 + 512]:
                         e.dma_start(out=o, in_=i), "ld", 16)
                ldx = P.now("ld")
                for o_ in range(8):
                    bk = (o_ % 2)
                    bwait("tensor", bk)
                    for zc in range(8):
                        P.op("tensor", lambda e, p=pb[bk][:, 0:512],
                             l=wbuf[:, 8192 + (o_ * 8 + zc) * 128:
                                    8192 + (o_ * 8 + zc) * 128 + 128],
                             r=zm(zc)[:, g * 512: g * 512 + 512],
                             s=(zc == 0), st=(zc == 7):
                             e.matmul(p, l, r, start=s, stop=st),
                             "pe" if zc == 7 else None, 1)
                    pev = P.now("pe")
                    P.wait("vector", "pe", pev)
                    if o_ == 0:
                        P.wait("vector", "ld", ldx)
                    dW = P.op("vector", lambda e,
                              o=xr[:, sl + o_ * 512: sl + (o_ + 1) * 512],
                              i0=pb[bk][:, 0:512],
                              i1=xr[:, sl + o_ * 512: sl + (o_ + 1) * 512]:
                              e.tensor_add(o, i0, i1), "dve", 1)
                    bankF[bk] = ("dve", dW)
                dver_g[g] = P.now("dve")
            if g >= 1:
                gg = g - 1
                track["dver"] = dver_g[gg]
                emit_ln_group(P, R, sc_ln, xr[:, (gg % 2) * 4096:(gg % 2) * 4096 + 4096],
                              x_dst, gg, track)
                apply_dve[gg] = track["dver"]

        P.flush(block)


def pack_l1_attn(inp):
    d = {}
    for nm, key in (("wqT", "L1_attn_Wq"), ("wkT", "L1_attn_Wk")):
        w = bf(np.asarray(inp[key], np.float32).T).reshape(8, 128, 1024)
        d[nm] = np.ascontiguousarray(w.transpose(1, 0, 2).reshape(128, 8192))
    d["wvp"] = pack_lhsT(bf(np.asarray(inp["L1_attn_Wv"], np.float32).T), 1024, 1024)
    d["wop"] = pack_lhsT(bf(np.asarray(inp["L1_attn_Wo"], np.float32).T), 1024, 1024)
    return d


L1A_DRAM = [
    ("wqT", [128, 8192], BF16), ("wkT", [128, 8192], BF16),
    ("wvp", [128, 8192], BF16), ("wop", [128, 8192], BF16),
]





def plan_l1_ffn(nc, P, block, R, x_src, x_dst):
    pb = R["pb"]
    mir = R["mir"]
    mt = lambda k: mir[:, k * TE:(k + 1) * TE]
    W = R["W"]
    TC = 512

    with ExitStack() as es:
        ent = es.enter_context
        bd1 = ent(nc.sbuf_tensor("n_bd1", [128, 2 * 128], BF16))
        bd2 = ent(nc.sbuf_tensor("n_bd2", [128, 2 * 512], BF16))
        hsb = ent(nc.sbuf_tensor("n_hsb", [128, 128 * TC], BF16))
        ysb = ent(nc.sbuf_tensor("n_ysb", [128, 4 * 512], BF16))
        wcp = ent(nc.sbuf_tensor("n_wcp", [128, 3 * 2048], BF16))
        xr = ent(nc.sbuf_tensor("n_xr", [128, 8 * 512], F32))
        rbf = ysb[:, 0:1024]
        rsq = ysb[:, 1024:2048]
        stat = ent(nc.sbuf_tensor("n_stat", [1, 2 * 512], F32))
        statb = ent(nc.sbuf_tensor("n_statb", [1, 2 * 512], BF16))
        osb = ent(nc.sbuf_tensor("n_osb", [128, 2 * 512], F32))

        P.wait("tensor", "dve", P.now("dve"))
        P.wait("tensor", "act", P.now("act"))
        P.wait("sync", "g", P.now("g"))
        ld0 = P.now("ld")
        P.wait("gpsimd", "pe", P.now("pe"))
        for k in range(8):
            P.op("gpsimd", lambda e, o=mt(k)[:, EXT_OFF:EXT_OFF + T],
                 i=x_src[k * 128:(k + 1) * 128, :]:
                 e.dma_start(out=o, in_=i), "g", 16)
        gmir = P.now("g")
        P.wait("tensor", "g", gmir)
        P.wait("tensor", "ld", ld0)

        sc_ln = {"rbf": rbf, "rsq": rsq, "stat": stat, "statb": statb,
                 "osb": osb}
        track = {"store_dves": [], "gstores": [], "dver": 0}
        bankF = {}
        def bwait(eng, bk):
            if bankF.get(bk):
                P.wait(eng, bankF[bk][0], bankF[bk][1])
        bd2_pe = {}
        wcp_pe = {}

        for t in range(4):
            c0 = EXT_OFF + t * TC
            P.wait("tensor", "dve", P.now("dve"))
            P.wait("tensor", "act", P.now("act"))
            for hg in range(32):
                sl4 = hg % 2
                if bd2_pe.get(sl4):
                    P.wait("sync", "pe", bd2_pe[sl4])
                P.op("sync", lambda e, o=bd1[:, sl4 * 128:(sl4 + 1) * 128],
                     i=W["bd1"][:, hg * 128:(hg + 1) * 128]:
                     e.dma_start(out=o, in_=i), "ld", 16)
                P.op("sync", lambda e, o=bd2[:, sl4 * 512:(sl4 + 1) * 512],
                     i=W["bd2"][:, hg * 512:(hg + 1) * 512]:
                     e.dma_start(out=o, in_=i), "ld", 16)
                ldb = P.now("ld")
                g8 = hg % 8
                yb = hg % 2
                bwait("tensor", yb)
                P.wait("tensor", "ld", ldb)
                P.op("tensor", lambda e, p=pb[yb][:, 0:TC],
                     l=bd1[:, sl4 * 128: sl4 * 128 + 128],
                     r=mt(g8)[:, c0:c0 + TC]:
                     e.matmul(p, l, r, start=True, stop=True), "pe", 1)
                pey = P.now("pe")
                P.wait("vector", "pe", pey)
                ysl = (hg % 2) * TC
                dY = P.op("vector", lambda e, o=ysb[:, ysl:ysl + TC],
                          i=pb[yb][:, 0:TC]:
                          e.tensor_copy(o, i), "dve", 1)
                bankF[yb] = ("dve", dY)
                P.wait("tensor", "ld", ldb)
                P.wait("tensor", "dve", dY)
                for qq in range(2):
                    zb = ("z", (hg * 2 + qq) % 2)
                    pz = R["pbig"][1] if zb[1] == 0 else R["pbig"][3]
                    bwait("tensor", zb)
                    for q in (2 * qq, 2 * qq + 1):
                        kc = hg * 4 + q
                        P.op("tensor", lambda e, p=pz[:, (q % 2) * TC:(q % 2) * TC + TC],
                             l=bd2[:, sl4 * 512 + q * 128: sl4 * 512 + q * 128 + 128],
                             r=ysb[:, ysl:ysl + TC]:
                             e.matmul(p, l, r, start=True, stop=True),
                             "pe" if q % 2 == 1 else None, 1)
                    pez = P.now("pe")
                    bd2_pe[sl4] = pez
                    P.wait("scalar", "pe", pez)
                    kc0 = hg * 4 + 2 * qq
                    aZ = P.op("scalar", lambda e,
                              o=hsb[:, kc0 * TC:(kc0 + 2) * TC],
                              i=pz[:, 0:2 * TC]:
                              e.activation(o, i, AF.Gelu), "act", 1)
                    bankF[zb] = ("act", aZ)
                if hg == 15:
                    act_half0 = P.now("act")
            act_h = P.now("act")
            P.wait("sync", "dve", track["dver"])
            for fc_ in range(8):
                P.op("sync", lambda e, o=xr[:, fc_ * 512:(fc_ + 1) * 512],
                     i=x_src[fc_ * 128:(fc_ + 1) * 128, t * 512: t * 512 + 512]:
                     e.dma_start(out=o, in_=i), "ld", 16)
            ldx = P.now("ld")
            P.wait("tensor", "act", act_half0)
            if t == 0:
                for bk in (4, 5):
                    for k32 in range(32):
                        P.op("tensor", lambda e, p=pb[bk][:, 0:512],
                             l=hsb[:, 0:128], r=hsb[:, 0:512],
                             s=(k32 == 0), st=(k32 == 31):
                             e.matmul(p, l, r, start=s, stop=st),
                             "pe" if k32 == 31 else None, 1)
            ldx_waited = [False]
            slabctr = [0]
            for q2 in range(2):
                P.wait("tensor", "act", act_half0 if q2 == 0 else act_h)
                for o_ in range(8):
                  for q4 in (2 * q2, 2 * q2 + 1):
                    ob = 4 + (o_ * 4 + q4) % 2
                    bwait("tensor", ob)
                    for e8 in range(2):
                        slab = q4 * 2 + e8
                        sl3 = slabctr[0] % 3
                        slabctr[0] += 1
                        if wcp_pe.get(sl3):
                            P.wait("sync", "pe", wcp_pe[sl3])
                        P.op("sync", lambda e, o=wcp[:, sl3 * 2048:(sl3 + 1) * 2048],
                             i=W["wcp"][:, o_ * 16384 + slab * 2048:
                                        o_ * 16384 + slab * 2048 + 2048]:
                             e.dma_start(out=o, in_=i), "ld", 16)
                        ldw = P.now("ld")
                        P.wait("tensor", "ld", ldw)
                        for k16 in range(16):
                            kk = e8 * 16 + k16
                            kc = q4 * 32 + kk
                            P.op("tensor", lambda e, p=pb[ob][:, 0:TC],
                                 l=wcp[:, sl3 * 2048 + k16 * 128:
                                        sl3 * 2048 + k16 * 128 + 128],
                                 r=hsb[:, kc * TC:(kc + 1) * TC],
                                 s=(kk == 0), st=(kk == 31):
                                 e.matmul(p, l, r, start=s, stop=st),
                                 "pe" if k16 == 15 else None, 1)
                        wcp_pe[sl3] = P.now("pe")
                    pev = P.now("pe")
                    P.wait("vector", "pe", pev)
                    if not ldx_waited[0]:
                        P.wait("vector", "ld", ldx)
                        ldx_waited[0] = True
                    dW = P.op("vector", lambda e,
                              o=xr[:, o_ * 512:(o_ + 1) * 512],
                              i0=pb[ob][:, 0:TC],
                              i1=xr[:, o_ * 512:(o_ + 1) * 512]:
                              e.tensor_add(o, i0, i1), "dve", 1)
                    bankF[ob] = ("dve", dW)
            track["dver"] = P.now("dve")
            emit_ln_group(P, R, sc_ln, xr, x_dst, t, track)

        P.flush(block)


def pack_l1_ffn(inp):
    d = {}
    rm = np.asarray(inp["L1_ffn_rail_maker"], np.float32)
    rup = np.asarray(inp["L1_ffn_rail_up_proj"], np.float32)
    bd1 = np.zeros((128, 32 * 128), np.float32)
    bd2 = np.zeros((128, 32 * 512), np.float32)
    for h in range(4):
        for g8 in range(8):
            for r in range(8):
                rg = 8 * g8 + r
                bd1[r * 16:(r + 1) * 16,
                    (h * 8 + g8) * 128 + r * 16:(h * 8 + g8) * 128 + r * 16 + 16] = rm[h, rg]
                bd2[r * 16:(r + 1) * 16,
                    (h * 8 + g8) * 512 + r * 64:(h * 8 + g8) * 512 + r * 64 + 64] = rup[h, rg].T
    d["bd1"] = bf(bd1)
    d["bd2"] = bf(bd2)
    d["wcp"] = pack_lhsT(bf(np.asarray(inp["L1_ffn_w_cproj"], np.float32).T),
                         16384, 1024, oc_outer=True)
    return d


L1F_DRAM = [
    ("bd1", [128, 32 * 128], BF16), ("bd2", [128, 32 * 512], BF16),
    ("wcp", [128, 128 * 1024], BF16),
]



def declare_common(nc):
    es = ExitStack()
    ent = es.enter_context
    R = {}
    sems = {}
    for name in ("ld", "g", "pe", "act", "dve", "pool"):
        sems[name] = ent(nc.semaphore(name))
    R["sems"] = sems
    pbig = []
    for i in range(4):
        pbig.append(ent(nc.psum_tensor(f"pbig{i}", [128, 1024], F32)))
    R["pbig"] = [p[:] for p in pbig]
    R["pb"] = [pbig[i // 2][:, (i % 2) * 512:(i % 2) * 512 + 512] for i in range(8)]
    R["pbf"] = pbig[3][:, 512:1024].bitcast(BF16)
    mir = ent(nc.sbuf_tensor("mir", [128, 8 * TE], BF16))
    R["mir"] = mir[:]
    ident = ent(nc.sbuf_tensor("c_ident", [128, 128], BF16))
    R["ident"] = ident[:]
    ones_row = ent(nc.sbuf_tensor("c_ones_row", [1, 128], BF16))
    R["ones_row"] = ones_row[:]
    st_a = ent(nc.sbuf_tensor("c_st_a", [128, 2], BF16))
    R["st_a"] = st_a[:]
    st_b = ent(nc.sbuf_tensor("c_st_b", [128, 2], BF16))
    R["st_b"] = st_b[:]
    fconst = ent(nc.sbuf_tensor("c_fconst", [128, 12], F32))
    R["fconst"] = fconst[:]
    return R, es


def load_common_consts(nc, P, R):
    W = R["W"]
    for dst, src in [(R["ident"], W["ident"]), (R["ones_row"], W["ones_row"]),
                     (R["st_a"], W["st_a"]), (R["st_b"], W["st_b"]),
                     (R["fconst"], W["fconst"])]:
        P.op("sync", lambda e, o=dst, i=src: e.dma_start(out=o, in_=i), "ld", 16)


def common_inputs():
    d = {}
    d["ident"] = bf(np.eye(128, dtype=np.float32))
    d["ones_row"] = bf(np.ones((1, 128), np.float32))
    sa = np.zeros((128, 2), np.float32); sa[:, 0] = 1.0
    sb = np.zeros((128, 2), np.float32); sb[:, 1] = 1.0
    d["st_a"] = bf(sa)
    d["st_b"] = bf(sb)
    fc = np.array([1e-8, 1e-5, 1.0 / 1024.0, 0.0, 1.0, -0.5, -1.0, 0.125,
                   -0.125, 0.0, 0.0, 0.0], np.float32)
    d["fconst"] = np.ascontiguousarray(np.tile(fc, (128, 1)))
    return d


def pack_l0_attn(inp):
    d = {}
    wqkv = np.asarray(inp["L0_attn_w_qkv"], np.float32)
    qk = wqkv[:, 0:2048]
    d["wqkv_qk"] = pack_lhsT(bf(qk), 1024, 2048, oc_outer=True)
    v = bf(wqkv[:, 2048:3072]).reshape(8, 128, 1024)
    d["wqkv_v"] = np.ascontiguousarray(v.transpose(1, 0, 2).reshape(128, 8 * 1024))
    d["proj"] = bf(np.asarray(inp["L0_attn_proj"], np.float32))
    d["wout"] = pack_lhsT(bf(np.asarray(inp["L0_attn_w_out"], np.float32)), 1024, 1024)
    d["b_out"] = np.ascontiguousarray(
        np.asarray(inp["L0_attn_b_out"], np.float32).reshape(8, 128).T)
    return d



def _finalize_outputs(P, block):
    P.wait("gpsimd", "g", P.now("g"))
    P.flush(block)


def build_l0_attn_program():
    nc = bass.Bass("TRN2", target_bir_lowering=False)
    x_src = nc.dram_tensor("xT", [D, TE], F32, kind="ExternalInput").ap()
    x_dst = nc.dram_tensor("x1", [D, T], F32, kind="ExternalOutput").ap()
    W = {}
    for name, shape, dt in L0A_DRAM + COMMON_DRAM:
        W[name] = nc.dram_tensor(name, shape, dt, kind="ExternalInput").ap()
    R, es = declare_common(nc)
    R["W"] = W
    with es:
        with nc.Block() as block:
            P = Prog(R["sems"])
            load_common_consts(nc, P, R)
            plan_l0_attn(nc, P, block, R, x_src, x_dst, es)
            _finalize_outputs(P, block)
    return nc


COMMON_DRAM = [
    ("ident", [128, 128], BF16), ("ones_row", [1, 128], BF16),
    ("st_a", [128, 2], BF16), ("st_b", [128, 2], BF16),
    ("fconst", [128, 12], F32),
]


def build_l0_ffn_program():
    nc = bass.Bass("TRN2", target_bir_lowering=False)
    x_src = nc.dram_tensor("x1", [D, T], F32, kind="ExternalInput").ap()
    x_dst = nc.dram_tensor("x2", [D, T], F32, kind="ExternalOutput").ap()
    W = {}
    for name, shape, dt in L0F_DRAM + COMMON_DRAM:
        W[name] = nc.dram_tensor(name, shape, dt, kind="ExternalInput").ap()
    R, es = declare_common(nc)
    R["W"] = W
    with es:
        with nc.Block() as block:
            P = Prog(R["sems"])
            load_common_consts(nc, P, R)
            plan_l0_ffn(nc, P, block, R, x_src, x_dst)
            _finalize_outputs(P, block)
    return nc


def build_l1_attn_program():
    nc = bass.Bass("TRN2", target_bir_lowering=False)
    x_src = nc.dram_tensor("x2", [D, T], F32, kind="ExternalInput").ap()
    x_dst = nc.dram_tensor("x3", [D, T], F32, kind="ExternalOutput").ap()
    z1cc_in = nc.dram_tensor("z1cc_in", [128, 1024], F32, kind="Internal").ap()
    z1cc_out = nc.dram_tensor("z1cc_out", [128, 1024], F32, kind="Internal").ap()
    W = {}
    for name, shape, dt in L1A_DRAM + COMMON_DRAM:
        W[name] = nc.dram_tensor(name, shape, dt, kind="ExternalInput").ap()
    R, es = declare_common(nc)
    R["W"] = W
    with es:
        with nc.Block() as block:
            P = Prog(R["sems"])
            load_common_consts(nc, P, R)
            plan_l1_attn(nc, P, block, R, x_src, x_dst, z1cc_in, z1cc_out)
            _finalize_outputs(P, block)
    return nc


def build_l1_ffn_program():
    nc = bass.Bass("TRN2", target_bir_lowering=False)
    x_src = nc.dram_tensor("x3", [D, T], F32, kind="ExternalInput").ap()
    x_dst = nc.dram_tensor("x4", [D, T], F32, kind="ExternalOutput").ap()
    W = {}
    for name, shape, dt in L1F_DRAM + COMMON_DRAM:
        W[name] = nc.dram_tensor(name, shape, dt, kind="ExternalInput").ap()
    R, es = declare_common(nc)
    R["W"] = W
    with es:
        with nc.Block() as block:
            P = Prog(R["sems"])
            load_common_consts(nc, P, R)
            plan_l1_ffn(nc, P, block, R, x_src, x_dst)
            _finalize_outputs(P, block)
    return nc


L0A_DRAM = [
    ("wqkv_qk", [128, 16 * 8 * 128], BF16), ("wqkv_v", [128, 8 * 1024], BF16),
    ("proj", [128, 64], BF16), ("wout", [128, 64 * 128], BF16),
    ("b_out", [128, 8], F32),
    ("mask_first", [128, 128], BF16), ("mask_mid", [128, 128], BF16),
    ("mask_ext_mid", [8, 128], BF16), ("mask_ext_last", [8, 128], BF16),
]


def build_fused_program():
    nc = bass.Bass("TRN2", target_bir_lowering=False)
    xT = nc.dram_tensor("xT", [D, TE], F32, kind="ExternalInput").ap()
    out = nc.dram_tensor("outT", [D, T], F32, kind="ExternalOutput").ap()
    x1 = nc.dram_tensor("x1s", [D, T], F32, kind="Internal").ap()
    x2 = nc.dram_tensor("x2s", [D, T], F32, kind="Internal").ap()
    x3 = nc.dram_tensor("x3s", [D, T], F32, kind="Internal").ap()
    z1cc_in = nc.dram_tensor("z1cc_in", [128, 1024], F32, kind="Internal").ap()
    z1cc_out = nc.dram_tensor("z1cc_out", [128, 1024], F32, kind="Internal").ap()
    W = {}
    for name, shape, dt in L0A_DRAM + L0F_DRAM + L1A_DRAM + L1F_DRAM + COMMON_DRAM:
        W[name] = nc.dram_tensor(name, shape, dt, kind="ExternalInput").ap()
    R, es = declare_common(nc)
    R["W"] = W
    with es:
        with nc.Block() as block:
            P = Prog(R["sems"])
            load_common_consts(nc, P, R)
            plan_l0_attn(nc, P, block, R, xT, x1, es)
            plan_l0_ffn(nc, P, block, R, x1, x2)
            plan_l1_attn(nc, P, block, R, x2, x3, z1cc_in, z1cc_out)
            plan_l1_ffn(nc, P, block, R, x3, out)
            _finalize_outputs(P, block)
    return nc


def kernel(**inputs):
    x = np.asarray(inputs["L0_attn_x"], np.float32)
    wall = {}
    wall.update(pack_l0_attn(inputs))
    wall.update(pack_l0_ffn(inputs))
    wall.update(pack_l1_attn(inputs))
    wall.update(pack_l1_ffn(inputs))
    wall.update(common_inputs())
    nc = build_fused_program()
    in_maps = []
    for c in range(NCORES):
        mf, mm, mem, mel = make_masks(c)
        m = {"xT": make_xT_ext(x, c),
             "mask_first": mf, "mask_mid": mm,
             "mask_ext_mid": mem, "mask_ext_last": mel}
        m.update(wall)
        in_maps.append(m)
    res = run_bass_kernel_spmd(nc, in_maps, core_ids=list(range(NCORES)))
    outs = [np.asarray(res.results[c]["outT"]).T for c in range(NCORES)]
    return np.stack(outs).reshape(2, 8192, 1024)


def kernel_unfused(**inputs):
    x = np.asarray(inputs["L0_attn_x"], np.float32)
    wl0a = pack_l0_attn(inputs)
    wl0f = pack_l0_ffn(inputs)
    com = common_inputs()

    nc = build_l0_attn_program()
    in_maps = []
    for c in range(NCORES):
        mf, mm, mem, mel = make_masks(c)
        m = {"xT": make_xT_ext(x, c),
             "mask_first": mf, "mask_mid": mm,
             "mask_ext_mid": mem, "mask_ext_last": mel}
        m.update(wl0a)
        m.update(com)
        in_maps.append(m)
    res = run_bass_kernel_spmd(nc, in_maps, core_ids=list(range(NCORES)))
    x1s = [np.asarray(res.results[c]["x1"]) for c in range(NCORES)]

    nc = build_l0_ffn_program()
    in_maps = []
    for c in range(NCORES):
        m = {"x1": x1s[c]}
        m.update(wl0f)
        m.update(com)
        in_maps.append(m)
    res = run_bass_kernel_spmd(nc, in_maps, core_ids=list(range(NCORES)))
    x2s = [np.asarray(res.results[c]["x2"]) for c in range(NCORES)]

    wl1a = pack_l1_attn(inputs)
    nc = build_l1_attn_program()
    in_maps = []
    for c in range(NCORES):
        m = {"x2": x2s[c]}
        m.update(wl1a)
        m.update(com)
        in_maps.append(m)
    res = run_bass_kernel_spmd(nc, in_maps, core_ids=list(range(NCORES)))
    x3s = [np.asarray(res.results[c]["x3"]) for c in range(NCORES)]

    wl1f = pack_l1_ffn(inputs)
    nc = build_l1_ffn_program()
    in_maps = []
    for c in range(NCORES):
        m = {"x3": x3s[c]}
        m.update(wl1f)
        m.update(com)
        in_maps.append(m)
    res = run_bass_kernel_spmd(nc, in_maps, core_ids=list(range(NCORES)))
    outs = [np.asarray(res.results[c]["x4"]).T for c in range(NCORES)]
    full = np.stack(outs).reshape(2, 8192, 1024)
    return full


# version 34
# speedup vs baseline: 1.0203x; 1.0203x over previous
import numpy as np
import ml_dtypes
import concourse.bass as bass
import concourse.mybir as mybir
import concourse.bass_utils as _bass_utils
from concourse.bass_utils import run_bass_kernel_spmd
from contextlib import ExitStack

if not getattr(_bass_utils, "_ldw_patch", False):
    _orig_run_command = _bass_utils.run_command

    def _run_command_ldw(cmd, *a, **kw):
        if isinstance(cmd, list):
            cmd = ["--enable-ldw-opt=true" if c == "--enable-ldw-opt=false" else c
                   for c in cmd]
        return _orig_run_command(cmd, *a, **kw)

    _bass_utils.run_command = _run_command_ldw
    _bass_utils._ldw_patch = True

F32 = mybir.dt.float32
BF16 = mybir.dt.bfloat16
AF = mybir.ActivationFunctionType
ALU = mybir.AluOpType
AX = mybir.AxisListType
BFNP = ml_dtypes.bfloat16

NCORES = 8
D = 1024
T = 2048
TE = 2176
EXT_OFF = 3
NC_CHUNK = 17
N_TCH = 16
LN_EPS = 1e-5

ENGS = ("sync", "tensor", "scalar", "vector", "gpsimd")


class Prog:

    def __init__(self, sems):
        self.q = {e: [] for e in ENGS}
        self.sems = sems
        self.cnt = {k: 0 for k in sems}

    def op(self, eng, fn, inc=None, n=1):
        if inc is None:
            self.q[eng].append(fn)
            return None
        self.cnt[inc] += n
        sem = self.sems[inc]
        self.q[eng].append(lambda e, fn=fn, sem=sem, n=n: fn(e).then_inc(sem, n))
        return self.cnt[inc]

    def wait(self, eng, name, val):
        sem = self.sems[name]
        self.q[eng].append(lambda e, sem=sem, val=val: e.wait_ge(sem, val))

    def now(self, name):
        return self.cnt[name]

    def flush(self, block):
        for eng in ENGS:
            q = self.q[eng]
            if q:
                def body(e, q=tuple(q)):
                    for fn in q:
                        fn(e)
                getattr(block, eng)(body)
                self.q[eng] = []


def bf(a):
    return np.asarray(a, np.float32).astype(BFNP)



def pack_lhsT(w_T, kdim, odim, oc_outer=True):
    K, OC = kdim // 128, odim // 128
    w = np.asarray(w_T).reshape(K, 128, OC, 128)
    if oc_outer:
        packed = w.transpose(1, 2, 0, 3).reshape(128, OC * K * 128)
    else:
        packed = w.transpose(1, 0, 2, 3).reshape(128, K * OC * 128)
    return np.ascontiguousarray(packed)


def make_xT_ext(x, c):
    b, q = divmod(c, 4)
    t0 = 2048 * q - 3
    buf = np.zeros((TE, D), np.float32)
    lo, hi = max(t0, 0), min(t0 + 2054, 8192)
    buf[lo - t0: hi - t0, :] = x[b, lo:hi, :]
    return np.ascontiguousarray(buf.T)


def make_masks(c):
    q = c % 4
    u = np.arange(128)[:, None]
    t = np.arange(128)[None, :]
    band = ((u - t >= 0) & (u - t <= 6)).astype(np.float32)
    mask_first = band.copy()
    if q == 0:
        mask_first[0:3, :] = 0.0
    e = np.arange(8)[:, None]
    ext = ((128 + e - t >= 0) & (128 + e - t <= 6)).astype(np.float32)
    ext[6:8, :] = 0.0
    mask_ext_last = ext.copy()
    if q == 3:
        mask_ext_last[3:6, :] = 0.0
    return bf(mask_first), bf(band), bf(ext), bf(mask_ext_last)




(C_EPS8, C_EPS5, C_INV1024, C_ZERO, C_ONE, C_NHALF, C_NONE, C_EIGHTH,
 C_NEIGHTH) = range(9)


def emit_ln_group(P, R, sc, rsb, x_dst, g, track, banks=(4, 0, 5, 6)):
    pb = R["pb"]
    FC = R["fconst"]
    fcv = lambda j, p0=0, p1=128: FC[p0:p1, j:j + 1]
    st_a, ones_row = R["st_a"], R["ones_row"]
    rbf, rsq, stat, statb = sc["rbf"], sc["rsq"], sc["stat"], sc["statb"]
    osb = sc["osb"]
    bS1, bS2, bB1, bB2 = banks
    store_dves, gstores = track["store_dves"], track["gstores"]
    dver = track["dver"]
    if track.get("st4a"):
        P.wait("tensor", "act", track["st4a"])
    if track.get("st0d"):
        P.wait("tensor", "dve", track["st0d"])
    pe_stats = []
    for o_ in range(8):
        sl = (o_ % 2) * 512
        if o_ >= 2:
            P.wait("vector", "pe", pe_stats[o_ - 2])
            P.wait("scalar", "pe", pe_stats[o_ - 2])
        P.op("vector", lambda e, o=rbf[:, sl:sl + 512],
             i=rsb[:, o_ * 512:(o_ + 1) * 512]:
             e.tensor_copy(o, i), "dve", 1)
        drb = P.now("dve")
        P.wait("scalar", "dve", dver)
        P.op("scalar", lambda e, o=rsq[:, sl:sl + 512],
             i=rsb[:, o_ * 512:(o_ + 1) * 512]:
             e.activation(o, i, AF.Square), "act", 1)
        arq = P.now("act")
        P.wait("tensor", "dve", drb)
        P.op("tensor", lambda e, p=pb[bS1][0:1, 0:512], l=st_a[:, 0:1],
             r=rbf[:, sl:sl + 512],
             s=(o_ == 0): e.matmul(p, l, r, start=s, stop=(o_ == 7)), None)
        P.wait("tensor", "act", arq)
        P.op("tensor", lambda e, p=pb[bS2][0:1, 0:512], l=st_a[:, 0:1],
             r=rsq[:, sl:sl + 512],
             s=(o_ == 0): e.matmul(p, l, r, start=s, stop=(o_ == 7)), "pe", 1)
        pe_stats.append(P.now("pe"))
    peS2 = P.now("pe")
    P.wait("scalar", "pe", peS2)
    P.op("scalar", lambda e, o=stat[:, 0:512], i=pb[bS1][0:1, 0:512],
         sc_=fcv(C_INV1024, 0, 1):
         e.activation(o, i, AF.Identity, scale=sc_), None)
    P.op("scalar", lambda e, o=stat[:, 512:1024], i=stat[:, 0:512]:
         e.activation(o, i, AF.Square), "act", 1)
    amsq = P.now("act")
    track["st4a"] = amsq
    P.wait("vector", "act", amsq)
    P.op("vector", lambda e, o=stat[:, 512:1024], i0=pb[bS2][0:1, 0:512],
         i1=stat[:, 512:1024], sc_=fcv(C_INV1024, 0, 1):
         e.scalar_tensor_tensor(o, i0, sc_, i1, ALU.mult, ALU.subtract),
         "dve", 1)
    dvar = P.now("dve")
    track["st0d"] = dvar
    P.wait("scalar", "dve", dvar)
    P.op("scalar", lambda e, o=stat[:, 512:1024], i=stat[:, 512:1024],
         b=fcv(C_EPS5, 0, 1):
         e.activation(o, i, AF.Ln, bias=b), None)
    P.op("scalar", lambda e, o=statb[:, 0:512], i=stat[:, 512:1024],
         sc_=fcv(C_NHALF, 0, 1):
         e.activation(o, i, AF.Exp, scale=sc_), None)
    P.op("scalar", lambda e, o=stat[:, 512:1024], i=stat[:, 512:1024],
         sc_=fcv(C_NHALF, 0, 1):
         e.activation(o, i, AF.Exp, scale=sc_), "act", 1)
    arstd = P.now("act")
    P.wait("vector", "act", arstd)
    P.op("vector", lambda e, o=statb[:, 512:1024], i0=stat[:, 0:512],
         i1=stat[:, 512:1024]: e.tensor_mul(o, i0, i1), "dve", 1)
    dmr = P.now("dve")
    P.wait("tensor", "dve", dmr)
    if track.get("bcast"):
        P.wait("tensor", "dve", track["bcast"])
    P.op("tensor", lambda e, p=pb[bB1][:, 0:512], l=ones_row[:], r=statb[:, 0:512]:
         e.matmul(p, l, r, start=True, stop=True), None)
    P.op("tensor", lambda e, p=pb[bB2][:, 0:512], l=ones_row[:], r=statb[:, 512:1024]:
         e.matmul(p, l, r, start=True, stop=True), "pe", 1)
    peB = P.now("pe")
    P.wait("vector", "pe", peB)
    for fc_ in range(8):
        j = len(store_dves)
        if j >= 2:
            P.wait("vector", "g", gstores[j - 2])
        osl = osb[:, (j % 2) * 512:(j % 2) * 512 + 512]
        P.op("vector", lambda e, o=osl, i0=rsb[:, fc_ * 512:(fc_ + 1) * 512],
             i1=pb[bB1][:, 0:512]: e.tensor_mul(o, i0, i1), None)
        P.op("vector", lambda e, o=osl, i0=osl, i1=pb[bB2][:, 0:512]:
             e.tensor_sub(o, i0, i1), "dve", 1)
        dx = P.now("dve")
        store_dves.append(dx)
        P.wait("gpsimd", "dve", dx)
        P.op("gpsimd", lambda e, o=x_dst[fc_ * 128:(fc_ + 1) * 128,
                                         g * 512: g * 512 + 512],
             i=osb[:, (j % 2) * 512:(j % 2) * 512 + 512]:
             e.dma_start(out=o, in_=i), "g", 16)
        gstores.append(P.now("g"))
    track["bcast"] = P.now("dve")
    track["dver"] = P.now("dve")



def plan_l0_attn(nc, P, block, R, x_src, x_dst, es_outer):
    pb = R["pb"]
    pbf = R["pbf"]
    mir = R["mir"]
    mt = lambda k: mir[:, k * TE:(k + 1) * TE]
    FC = R["fconst"]
    fcv = lambda j, p0=0, p1=128: FC[p0:p1, j:j + 1]

    with ExitStack() as es:
        ent = es.enter_context
        wqp = ent(nc.sbuf_tensor("a_wqp", [128, 2 * 4 * 1024], BF16))
        wvo = ent(nc.sbuf_tensor("a_wvo", [128, 8 * 1024], BF16))
        wv = wvo
        wo = wvo
        proj = ent(nc.sbuf_tensor("a_proj", [128, 64], BF16))
        bout = ent(nc.sbuf_tensor("a_bout", [128, 8], F32))
        mks = ent(nc.sbuf_tensor("a_mks", [128, 256], BF16))
        mke = ent(nc.sbuf_tensor("a_mke", [8, 256], BF16))
        vtok = ent(nc.sbuf_tensor("a_vtok", [128, NC_CHUNK * 1026], BF16))
        phi = ent(nc.sbuf_tensor("a_phi", [128, 2 * TE], BF16))
        qks = ent(nc.sbuf_tensor("a_qks", [128, 2 * 512], BF16))
        er = ent(nc.sbuf_tensor("a_er", [128, 2 * 512], F32))
        ssb = ent(nc.sbuf_tensor("a_ssb", [128, 2 * 128], BF16))
        sse = ent(nc.sbuf_tensor("a_sse", [8, 2 * 128], BF16))
        zsb = ent(nc.sbuf_tensor("a_zsb", [128, 4], F32))
        atn = ent(nc.sbuf_tensor("a_atn", [128, 2 * 128], BF16))
        mbf = ent(nc.sbuf_tensor("a_mbf", [128, 8 * T], BF16))
        rbf = ent(nc.sbuf_tensor("a_rbf", [128, 2 * 512], BF16))
        rsq = ent(nc.sbuf_tensor("a_rsq", [128, 2 * 512], BF16))
        xf32 = ent(nc.sbuf_tensor("a_xf32", [128, 2 * 8 * 512], F32))
        srow = ent(nc.sbuf_tensor("a_srow", [2, 512], F32))
        stat = ent(nc.sbuf_tensor("a_stat", [1, 2 * 512], F32))
        statb = ent(nc.sbuf_tensor("a_statb", [1, 2 * 512], BF16))
        osb = ent(nc.sbuf_tensor("a_osb", [128, 2 * 512], F32))

        vt = lambda c: vtok[:, c * 1026: c * 1026 + 1026]

        W = R["W"]
        for dst, src in [
            (wv[:], W["wqkv_v"]),
            (proj[:], W["proj"]), (bout[:], W["b_out"]),
            (mks[:, 0:128], W["mask_first"]), (mks[:, 128:256], W["mask_mid"]),
            (mke[:, 0:128], W["mask_ext_mid"]), (mke[:, 128:256], W["mask_ext_last"]),
        ]:
            P.op("sync", lambda e, o=dst, i=src: e.dma_start(out=o, in_=i), "ld", 16)
        ld0 = P.now("ld")

        for k in range(8):
            P.op("gpsimd", lambda e, o=mt(k), i=x_src[k * 128:(k + 1) * 128, :]:
                 e.dma_start(out=o, in_=i), "g", 16)
        gmir = P.now("g")

        for c in range(NC_CHUNK):
            P.op("vector", lambda e, o=vt(c)[:, 1024:1025]: e.memset(o, 1.0), "dve", 1)

        bank_free_act = {}
        P.wait("tensor", "ld", ld0)
        P.wait("tensor", "g", gmir)
        for c in range(NC_CHUNK):
            for half in range(2):
                bk = (2 * c + half) % 2
                if (c, half) in ():
                    pass
                if bank_free_act.get(bk):
                    P.wait("tensor", "act", bank_free_act[bk])
                for k in range(8):
                    P.op("tensor", lambda e, p=pb[bk][:, 0:512],
                         l=mt(k)[:, c * 128:(c + 1) * 128],
                         r=wv[:, k * 1024 + half * 512: k * 1024 + half * 512 + 512],
                         s=(k == 0), t=(k == 7):
                         e.matmul(p, l, r, start=s, stop=t),
                         "pe" if k == 7 else None, 1)
                pev = P.now("pe")
                P.wait("scalar", "pe", pev)
                bank_free_act[bk] = P.op(
                    "scalar", lambda e, o=vt(c)[:, half * 512: half * 512 + 512],
                    i=pb[bk][:, 0:512]: e.activation(o, i, AF.Identity), "act", 1)

        P.wait("sync", "pe", P.now("pe"))
        P.op("sync", lambda e, o=wo[:], i=W["wout"]: e.dma_start(out=o, in_=i), "ld", 16)
        ld_wo = P.now("ld")

        last_fin = 0
        wq_pe = {}
        for hp in range(4):
            if hp > 0:
                P.wait("tensor", "dve", P.now("dve"))
            wslot = (hp % 2) * 4096
            if wq_pe.get(hp % 2):
                P.wait("sync", "pe", wq_pe[hp % 2])
            for j, oc in enumerate((2 * hp, 2 * hp + 1, 8 + 2 * hp, 9 + 2 * hp)):
                P.op("sync", lambda e, o=wqp[:, wslot + j * 1024: wslot + j * 1024 + 1024],
                     i=W["wqkv_qk"][:, oc * 1024:(oc + 1) * 1024]:
                     e.dma_start(out=o, in_=i), "ld", 16)
            ldqk = P.now("ld")
            P.wait("tensor", "ld", ldqk)
            for src_i, ocg in ((0, (2 * hp, 2 * hp + 1)), (1, (8 + 2 * hp, 9 + 2 * hp))):
                phv = phi[:, src_i * TE:(src_i + 1) * TE]
                for sub in range(5):
                    w = 512 if sub < 4 else 128
                    c0 = sub * 512
                    for ocx in range(2):
                        bk = 2 + ocx
                        if bank_free_act.get(bk):
                            P.wait("tensor", "act", bank_free_act[bk])
                        oc = ocg[ocx]
                        j = src_i * 2 + ocx
                        for k in range(8):
                            P.op("tensor", lambda e, p=pb[bk][:, 0:w],
                                 l=wqp[:, wslot + j * 1024 + k * 128:
                                        wslot + j * 1024 + k * 128 + 128],
                                 r=mt(k)[:, c0:c0 + w],
                                 s=(k == 0), t=(k == 7):
                                 e.matmul(p, l, r, start=s, stop=t),
                                 "pe" if k == 7 else None, 1)
                        pev = P.now("pe")
                        wq_pe[hp % 2] = pev
                        P.wait("scalar", "pe", pev)
                        bank_free_act[bk] = P.op(
                            "scalar", lambda e, o=qks[:, ocx * 512: ocx * 512 + w],
                            i=pb[bk][:, 0:w]: e.activation(o, i, AF.Identity), "act", 1)
                    aq = P.now("act")
                    if bank_free_act.get(4):
                        P.wait("tensor", "dve", bank_free_act[4])
                    P.wait("tensor", "act", aq)
                    for ocx in range(2):
                        P.op("tensor", lambda e, p=pb[4][ocx * 64:(ocx + 1) * 64, 0:w],
                             l=proj[:], r=qks[:, ocx * 512: ocx * 512 + w]:
                             e.matmul(p, l, r, start=True, stop=True),
                             "pe" if ocx == 1 else None, 1)
                    pev = P.now("pe")
                    P.wait("scalar", "pe", pev)
                    P.wait("scalar", "dve", last_fin)
                    P.op("scalar", lambda e, o=er[:, 0:w], i=pb[4][:, 0:w]:
                         e.activation(o, i, AF.Exp), "act", 1)
                    aE = P.now("act")
                    P.wait("vector", "act", aE)
                    P.op("vector", lambda e, o=er[:, 512:512 + w], i=pb[4][:, 0:w]:
                         e.tensor_scalar_max(o, i, 0.0), "dve", 1)
                    dmx = P.now("dve")
                    P.wait("vector", "dve", dmx)
                    fin = P.op("vector", lambda e, o=phv[:, c0:c0 + w],
                               i0=er[:, 0:w], i1=er[:, 512:512 + w], fc2=fcv(C_ONE):
                               e.scalar_tensor_tensor(o, i0, fc2, i1, ALU.min, ALU.add),
                               "dve", 1)
                    last_fin = fin
                    bank_free_act[4] = fin

            qph = phi[:, 0:TE]
            kph = phi[:, TE:2 * TE]
            dve_phi = P.now("dve")

            atn_free = {}
            sb_free = {}
            tr_free = {}
            peT_prev = 0
            P.wait("tensor", "dve", dve_phi)
            for c in range(N_TCH):
                sbk = 2 + (c % 2)
                abk = 5 + (c % 2)
                if sb_free.get(sbk):
                    P.wait("tensor", "dve", sb_free[sbk])
                for hx in range(2):
                    pr = 64 * hx
                    P.op("tensor", lambda e, p=pb[sbk][:, hx * 128: hx * 128 + 128],
                         l=kph[pr:pr + 64, c * 128: c * 128 + 128],
                         r=qph[pr:pr + 64, EXT_OFF + c * 128: EXT_OFF + c * 128 + 128]:
                         e.matmul(p, l, r, start=True, stop=True), None)
                    P.op("tensor", lambda e, p=pb[sbk][0:6, 256 + hx * 128: 256 + hx * 128 + 128],
                         l=kph[pr:pr + 64, c * 128 + 128: c * 128 + 134],
                         r=qph[pr:pr + 64, EXT_OFF + c * 128: EXT_OFF + c * 128 + 128]:
                         e.matmul(p, l, r, start=True, stop=True),
                         "pe" if hx == 1 else None, 1)
                peS = P.now("pe")
                P.wait("vector", "pe", peS)
                mm = mks[:, 0:128] if c == 0 else mks[:, 128:256]
                me = mke[0:6, 128:256] if c == N_TCH - 1 else mke[0:6, 0:128]
                for hx in range(2):
                    P.op("vector", lambda e, o=ssb[:, hx * 128:(hx + 1) * 128],
                         i0=pb[sbk][:, hx * 128: hx * 128 + 128], i1=mm:
                         e.tensor_mul(o, i0, i1), None)
                    P.op("vector", lambda e, o=sse[0:6, hx * 128: hx * 128 + 128],
                         i0=pb[sbk][0:6, 256 + hx * 128: 256 + hx * 128 + 128], i1=me:
                         e.tensor_mul(o, i0, i1), "dve", 1)
                sb_free[sbk] = P.now("dve")
                dveS = P.now("dve")
                if atn_free.get(abk):
                    P.wait("tensor", atn_free[abk][0], atn_free[abk][1])
                P.wait("tensor", "dve", dveS)
                for hx in range(2):
                    h = 2 * hp + hx
                    P.op("tensor", lambda e, p=pb[abk][:, hx * 160: hx * 160 + 128],
                         l=ssb[:, hx * 128:(hx + 1) * 128],
                         r=vt(c)[:, h * 128: h * 128 + 128]:
                         e.matmul(p, l, r, start=True, stop=False), None)
                    P.op("tensor", lambda e, p=pb[abk][:, hx * 160: hx * 160 + 128],
                         l=sse[0:6, hx * 128: hx * 128 + 128],
                         r=vt(c + 1)[0:6, h * 128: h * 128 + 128]:
                         e.matmul(p, l, r, start=False, stop=True), None)
                    P.op("tensor", lambda e, p=pb[abk][:, hx * 160 + 128: hx * 160 + 129],
                         l=ssb[:, hx * 128:(hx + 1) * 128],
                         r=vt(c)[:, 1024:1025]:
                         e.matmul(p, l, r, start=True, stop=False), None)
                    P.op("tensor", lambda e, p=pb[abk][:, hx * 160 + 128: hx * 160 + 129],
                         l=sse[0:6, hx * 128: hx * 128 + 128],
                         r=vt(c + 1)[0:6, 1024:1025]:
                         e.matmul(p, l, r, start=False, stop=True),
                         "pe" if hx == 1 else None, 1)
                peA = P.now("pe")
                P.wait("vector", "pe", peA)
                for hx in range(2):
                    P.op("vector", lambda e, o=zsb[:, hx:hx + 1],
                         i=pb[abk][:, hx * 160 + 128: hx * 160 + 129]:
                         e.tensor_copy(o, i), "dve", 1)
                dz = P.now("dve")
                P.wait("scalar", "dve", dz)
                P.op("scalar", lambda e, o=zsb[:, 2:4], i=zsb[:, 0:2], b=fcv(C_EPS8):
                     e.activation(o, i, AF.Ln, bias=b), None)
                P.op("scalar", lambda e, o=zsb[:, 2:4], i=zsb[:, 2:4], sc=fcv(C_NONE):
                     e.activation(o, i, AF.Exp, scale=sc), "act", 1)
                az = P.now("act")
                P.wait("scalar", "pe", peT_prev)
                for hx in range(2):
                    P.op("scalar", lambda e, o=atn[:, hx * 128:(hx + 1) * 128],
                         i=pb[abk][:, hx * 160: hx * 160 + 128],
                         s=zsb[:, 2 + hx:3 + hx]:
                         e.activation(o, i, AF.Identity, scale=s), "act", 1)
                atn_free[abk] = ("act", P.now("act"))
                actA = P.now("act")
                if tr_free.get(c % 2):
                    P.wait("tensor", tr_free[c % 2][0], tr_free[c % 2][1])
                P.wait("tensor", "act", actA)
                for hx in range(2):
                    P.op("tensor", lambda e, p=pbf[:, (c % 2) * 256 + hx * 128:
                                                    (c % 2) * 256 + (hx + 1) * 128],
                         i=atn[:, hx * 128:(hx + 1) * 128], idn=R["ident"]:
                         e.transpose(p, i, idn), "pe" if hx == 1 else None, 1)
                peT = P.now("pe")
                P.wait("scalar", "pe", peT)
                peT_prev = peT
                for hx in range(2):
                    h = 2 * hp + hx
                    P.op("scalar", lambda e, o=mbf[:, h * T + c * 128: h * T + c * 128 + 128],
                         i=pbf[:, (c % 2) * 256 + hx * 128:(c % 2) * 256 + (hx + 1) * 128]:
                         e.copy(o, i), "act", 1)
                tr_free[c % 2] = ("act", P.now("act"))

        dve_attn_done = P.now("dve")

        st_a, st_b, ones_row = R["st_a"], R["st_b"], R["ones_row"]
        P.wait("tensor", "dve", dve_attn_done)
        P.wait("tensor", "act", P.now("act"))
        P.wait("tensor", "ld", ld_wo)
        sc_ln = {"rbf": rbf, "rsq": rsq, "stat": stat, "statb": statb,
                 "osb": osb}
        track = {"store_dves": [], "gstores": [], "dver": 0}
        dver_g = {}
        apply_dve = {}
        for g in range(5):
            if g < 4:
                sl = (g % 2) * 4096
                P.wait("sync", "dve", apply_dve.get(g - 2, 0))
                for fc in range(8):
                    P.op("sync", lambda e, o=xf32[:, sl + fc * 512: sl + (fc + 1) * 512],
                         i=x_src[fc * 128:(fc + 1) * 128,
                                 EXT_OFF + g * 512: EXT_OFF + g * 512 + 512]:
                         e.dma_start(out=o, in_=i), "ld", 16)
                ldx = P.now("ld")
                for o_ in range(8):
                    bk = 2 + (o_ % 2)
                    if bank_free_act.get(("wo", bk)):
                        P.wait("tensor", "dve", bank_free_act[("wo", bk)])
                    for h in range(8):
                        P.op("tensor", lambda e, p=pb[bk][:, 0:512],
                             l=wo[:, (o_ * 8 + h) * 128:(o_ * 8 + h) * 128 + 128],
                             r=mbf[:, h * T + g * 512: h * T + g * 512 + 512],
                             s=(h == 0), t=(h == 7):
                             e.matmul(p, l, r, start=s, stop=t),
                             "pe" if h == 7 else None, 1)
                    pev = P.now("pe")
                    P.wait("vector", "pe", pev)
                    if o_ == 0:
                        P.wait("vector", "ld", ldx)
                    bank_free_act[("wo", bk)] = P.op(
                        "vector", lambda e, o=xf32[:, sl + o_ * 512: sl + (o_ + 1) * 512],
                        i=pb[bk][:, 0:512], s=bout[:, o_:o_ + 1],
                        x=xf32[:, sl + o_ * 512: sl + (o_ + 1) * 512]:
                        e.scalar_tensor_tensor(o, i, s, x, ALU.add, ALU.add), "dve", 1)
                dver_g[g] = P.now("dve")
            if g >= 1:
                gg = g - 1
                track["dver"] = dver_g[gg]
                emit_ln_group(P, R, sc_ln, xf32[:, (gg % 2) * 4096:(gg % 2) * 4096 + 4096],
                              x_dst, gg, track)
                apply_dve[gg] = track["dver"]

        P.flush(block)





def plan_l0_ffn(nc, P, block, R, x_src, x_dst):
    pb = R["pb"]
    mir = R["mir"]
    mt = lambda k: mir[:, k * TE:(k + 1) * TE]
    W = R["W"]

    with ExitStack() as es:
        ent = es.enter_context
        Aa = ent(nc.sbuf_tensor("f_Aa", [128, 32 * 512], BF16))
        Ab = ent(nc.sbuf_tensor("f_Ab", [128, 32 * 512], BF16))
        Ba = ent(nc.sbuf_tensor("f_Ba", [128, 32 * 512], BF16))
        Bb = ent(nc.sbuf_tensor("f_Bb", [128, 32 * 512], BF16))
        w12 = ent(nc.sbuf_tensor("f_w12", [128, 2 * 2048], BF16))
        hw = ent(nc.sbuf_tensor("f_hw", [128, 2 * 4096], BF16))
        stat = ent(nc.sbuf_tensor("f_stat", [1, 2 * 512], F32))
        scl = ent(nc.sbuf_tensor("f_scl", [128, 136], F32))
        xr = ent(nc.sbuf_tensor("f_xr", [128, 8 * 512], F32))
        rbf = Bb[:, 0:1024]
        rsq = Bb[:, 1024:2048]
        statb = Bb[0:1, 4096:5120]
        osb = Bb[:, 5120:7168].bitcast(F32)
        sil = xr[:, 0:1024]

        P.wait("tensor", "dve", P.now("dve"))
        P.wait("tensor", "act", P.now("act"))
        P.wait("sync", "g", P.now("g"))
        P.op("sync", lambda e, o=scl[:], i=W["l0f_scales"]:
             e.dma_start(out=o, in_=i), "ld", 16)
        P.wait("gpsimd", "pe", P.now("pe"))
        for k in range(8):
            P.op("gpsimd", lambda e, o=mt(k)[:, EXT_OFF:EXT_OFF + T],
                 i=x_src[k * 128:(k + 1) * 128, :]:
                 e.dma_start(out=o, in_=i), "g", 16)
        gmir = P.now("g")
        ld0 = P.now("ld")
        P.wait("tensor", "g", gmir)
        P.wait("tensor", "ld", ld0)

        sc_ln = {"rbf": rbf, "rsq": rsq, "stat": stat, "statb": statb,
                 "osb": osb}
        track = {"store_dves": [], "gstores": [], "dver": 0}
        slab_pe = {}
        bankF = {}

        def bwait(eng, bk):
            if bankF.get(bk):
                P.wait(eng, bankF[bk][0], bankF[bk][1])

        for tp in range(2):
            cA = EXT_OFF + (2 * tp) * 512
            cB = cA + 512
            P.wait("tensor", "dve", P.now("dve"))
            P.wait("tensor", "act", P.now("act"))
            for oc in range(32):
                slot = oc % 2
                b0 = slot * 4
                if slab_pe.get(("w12", slot)):
                    P.wait("sync", "pe", slab_pe[("w12", slot)])
                P.op("sync", lambda e, o=w12[:, slot * 2048: slot * 2048 + 1024],
                     i=W["w1p"][:, oc * 1024:(oc + 1) * 1024]:
                     e.dma_start(out=o, in_=i), "ld", 16)
                P.op("sync", lambda e, o=w12[:, slot * 2048 + 1024: slot * 2048 + 2048],
                     i=W["w2p"][:, oc * 1024:(oc + 1) * 1024]:
                     e.dma_start(out=o, in_=i), "ld", 16)
                ldw = P.now("ld")
                P.wait("tensor", "ld", ldw)
                for bk in range(b0, b0 + 4):
                    bwait("tensor", bk)
                for k in range(8):
                    for i2, cc in ((0, cA), (1, cB)):
                        P.op("tensor", lambda e, p=pb[b0 + i2][:, 0:512],
                             l=w12[:, slot * 2048 + k * 128: slot * 2048 + k * 128 + 128],
                             r=mt(k)[:, cc:cc + 512],
                             s=(k == 0), st=(k == 7):
                             e.matmul(p, l, r, start=s, stop=st),
                             "pe" if (k == 7 and i2 == 1) else None, 1)
                pe1 = P.now("pe")
                for k in range(8):
                    for i2, cc in ((0, cA), (1, cB)):
                        P.op("tensor", lambda e, p=pb[b0 + 2 + i2][:, 0:512],
                             l=w12[:, slot * 2048 + 1024 + k * 128:
                                    slot * 2048 + 1024 + k * 128 + 128],
                             r=mt(k)[:, cc:cc + 512],
                             s=(k == 0), st=(k == 7):
                             e.matmul(p, l, r, start=s, stop=st),
                             "pe" if (k == 7 and i2 == 1) else None, 1)
                pe2 = P.now("pe")
                slab_pe[("w12", slot)] = pe2
                P.wait("scalar", "pe", pe1)
                P.wait("scalar", "dve", slab_pe.get(("sil",), 0))
                for i2 in range(2):
                    aS = P.op("scalar", lambda e,
                              o=sil[:, i2 * 512: i2 * 512 + 512],
                              i=pb[b0 + i2][:, 0:512], sc_=scl[:, oc:oc + 1]:
                              e.activation(o, i, AF.Silu, scale=sc_), "act", 1)
                    bankF[b0 + i2] = ("act", aS)
                aS2 = P.now("act")
                P.wait("vector", "pe", pe2)
                P.wait("vector", "act", aS2)
                for i2, dst in ((0, Aa), (1, Ab)):
                    dA = P.op("vector", lambda e, o=dst[:, oc * 512:(oc + 1) * 512],
                              i0=pb[b0 + 2 + i2][:, 0:512], s_=scl[:, 32 + oc:33 + oc],
                              i1=sil[:, i2 * 512: i2 * 512 + 512]:
                              e.scalar_tensor_tensor(o, i0, s_, i1, ALU.mult, ALU.mult),
                              "dve", 1)
                    bankF[b0 + 2 + i2] = ("dve", dA)
                slab_pe[("sil",)] = P.now("dve")
            dve_A = P.now("dve")
            P.wait("tensor", "dve", dve_A)
            for oc in range(32):
                slot = oc % 2
                b0 = slot * 2
                if slab_pe.get(("hw", slot)):
                    P.wait("sync", "pe", slab_pe[("hw", slot)])
                P.op("sync", lambda e, o=hw[:, slot * 4096:(slot + 1) * 4096],
                     i=W["h1p"][:, oc * 4096:(oc + 1) * 4096]:
                     e.dma_start(out=o, in_=i), "ld", 16)
                ldw = P.now("ld")
                P.wait("tensor", "ld", ldw)
                bwait("tensor", b0)
                bwait("tensor", b0 + 1)
                for k in range(32):
                    for i2, src in ((0, Aa), (1, Ab)):
                        P.op("tensor", lambda e, p=pb[b0 + i2][:, 0:512],
                             l=hw[:, slot * 4096 + k * 128: slot * 4096 + k * 128 + 128],
                             r=src[:, k * 512:(k + 1) * 512],
                             s=(k == 0), st=(k == 31):
                             e.matmul(p, l, r, start=s, stop=st),
                             "pe" if (k == 31 and i2 == 1) else None, 1)
                peh = P.now("pe")
                slab_pe[("hw", slot)] = peh
                P.wait("scalar", "pe", peh)
                for i2, dst in ((0, Ba), (1, Bb)):
                    aG = P.op("scalar", lambda e, o=dst[:, oc * 512:(oc + 1) * 512],
                              i=pb[b0 + i2][:, 0:512], sc_=scl[:, 64 + oc:65 + oc]:
                              e.activation(o, i, AF.Gelu, scale=sc_), "act", 1)
                    bankF[b0 + i2] = ("act", aG)
            act_B = P.now("act")
            P.wait("tensor", "act", act_B)
            for oc in range(32):
                slot = oc % 2
                b0 = 4 + slot * 2
                if slab_pe.get(("hw2", slot)):
                    P.wait("sync", "pe", slab_pe[("hw2", slot)])
                P.op("sync", lambda e, o=hw[:, slot * 4096:(slot + 1) * 4096],
                     i=W["h2p"][:, oc * 4096:(oc + 1) * 4096]:
                     e.dma_start(out=o, in_=i), "ld", 16)
                ldw = P.now("ld")
                P.wait("tensor", "ld", ldw)
                bwait("tensor", b0)
                bwait("tensor", b0 + 1)
                for k in range(32):
                    for i2, src in ((0, Ba), (1, Bb)):
                        P.op("tensor", lambda e, p=pb[b0 + i2][:, 0:512],
                             l=hw[:, slot * 4096 + k * 128: slot * 4096 + k * 128 + 128],
                             r=src[:, k * 512:(k + 1) * 512],
                             s=(k == 0), st=(k == 31):
                             e.matmul(p, l, r, start=s, stop=st),
                             "pe" if (k == 31 and i2 == 1) else None, 1)
                peh = P.now("pe")
                slab_pe[("hw2", slot)] = peh
                P.wait("scalar", "pe", peh)
                for i2, dst in ((0, Aa), (1, Ab)):
                    aG = P.op("scalar", lambda e, o=dst[:, oc * 512:(oc + 1) * 512],
                              i=pb[b0 + i2][:, 0:512], sc_=scl[:, 96 + oc:97 + oc]:
                              e.activation(o, i, AF.Gelu, scale=sc_), "act", 1)
                    bankF[b0 + i2] = ("act", aG)
            act_C = P.now("act")
            P.wait("tensor", "act", act_C)
            for i2, src in ((0, Aa), (1, Ab)):
                t = 2 * tp + i2
                P.wait("sync", "dve", track["dver"])
                for fc_ in range(8):
                    P.op("sync", lambda e, o=xr[:, fc_ * 512:(fc_ + 1) * 512],
                         i=x_src[fc_ * 128:(fc_ + 1) * 128, t * 512: t * 512 + 512]:
                         e.dma_start(out=o, in_=i), "ld", 16)
                ldx = P.now("ld")
                for o_ in range(8):
                    slot = o_ % 2
                    if slab_pe.get(("w3", slot, i2)):
                        P.wait("sync", "pe", slab_pe[("w3", slot, i2)])
                    P.op("sync", lambda e, o=hw[:, slot * 4096:(slot + 1) * 4096],
                         i=W["w3p"][:, o_ * 4096:(o_ + 1) * 4096]:
                         e.dma_start(out=o, in_=i), "ld", 16)
                    ldw = P.now("ld")
                    P.wait("tensor", "ld", ldw)
                    bwait("tensor", slot)
                    for k in range(32):
                        P.op("tensor", lambda e, p=pb[slot][:, 0:512],
                             l=hw[:, slot * 4096 + k * 128: slot * 4096 + k * 128 + 128],
                             r=src[:, k * 512:(k + 1) * 512],
                             s=(k == 0), st=(k == 31):
                             e.matmul(p, l, r, start=s, stop=st),
                             "pe" if k == 31 else None, 1)
                    pew = P.now("pe")
                    slab_pe[("w3", slot, i2)] = pew
                    P.wait("vector", "pe", pew)
                    if o_ == 0:
                        P.wait("vector", "ld", ldx)
                    dW = P.op("vector", lambda e, o=xr[:, o_ * 512:(o_ + 1) * 512],
                              i0=pb[slot][:, 0:512], s_=scl[:, 128 + o_:129 + o_],
                              x=xr[:, o_ * 512:(o_ + 1) * 512]:
                              e.scalar_tensor_tensor(o, i0, s_, x, ALU.mult, ALU.add),
                              "dve", 1)
                    bankF[slot] = ("dve", dW)
                track["dver"] = P.now("dve")
                emit_ln_group(P, R, sc_ln, xr, x_dst, t, track)
                P.wait("tensor", "dve", P.now("dve"))
                P.wait("tensor", "act", P.now("act"))

        P.flush(block)


def pack_l0_ffn(inp):
    d = {}
    d["w1p"] = pack_lhsT(bf(np.asarray(inp["L0_ffn_w1_w"], np.float32).T), 1024, 4096)
    d["w2p"] = pack_lhsT(bf(np.asarray(inp["L0_ffn_w2_w"], np.float32).T), 1024, 4096)
    d["h1p"] = pack_lhsT(bf(np.asarray(inp["L0_ffn_h1_w"], np.float32).T), 4096, 4096)
    d["h2p"] = pack_lhsT(bf(np.asarray(inp["L0_ffn_h2_w"], np.float32).T), 4096, 4096)
    d["w3p"] = pack_lhsT(bf(np.asarray(inp["L0_ffn_w3_w"], np.float32).T), 4096, 1024)
    scl = np.zeros((128, 136), np.float32)
    scl[:, 0:32] = np.asarray(inp["L0_ffn_w1_s"], np.float32).reshape(32, 128).T
    scl[:, 32:64] = np.asarray(inp["L0_ffn_w2_s"], np.float32).reshape(32, 128).T
    scl[:, 64:96] = np.asarray(inp["L0_ffn_h1_s"], np.float32).reshape(32, 128).T
    scl[:, 96:128] = np.asarray(inp["L0_ffn_h2_s"], np.float32).reshape(32, 128).T
    scl[:, 128:136] = np.asarray(inp["L0_ffn_w3_s"], np.float32).reshape(8, 128).T
    d["l0f_scales"] = np.ascontiguousarray(scl)
    return d


L0F_DRAM = [
    ("w1p", [128, 32 * 1024], BF16), ("w2p", [128, 32 * 1024], BF16),
    ("h1p", [128, 32 * 4096], BF16), ("h2p", [128, 32 * 4096], BF16),
    ("w3p", [128, 8 * 4096], BF16), ("l0f_scales", [128, 136], F32),
]





def plan_l1_attn(nc, P, block, R, x_src, x_dst, z1cc_in, z1cc_out):
    pb = R["pb"]
    mir = R["mir"]
    mt = lambda k: mir[:, k * TE:(k + 1) * TE]
    W = R["W"]
    FC = R["fconst"]
    fcv = lambda j, p0=0, p1=128: FC[p0:p1, j:j + 1]

    with ExitStack() as es:
        ent = es.enter_context
        wbuf = ent(nc.sbuf_tensor("l_wbuf", [128, 2 * 8192], BF16))
        arena = ent(nc.sbuf_tensor("l_arena", [128, 32 * 1024], BF16))
        z1f = ent(nc.sbuf_tensor("l_z1f", [128, 1024], F32))
        z2f = ent(nc.sbuf_tensor("l_z2f", [128, 1024], F32))
        z2b = ent(nc.sbuf_tensor("l_z2b", [128, 8 * 128], BF16))
        mx = ent(nc.sbuf_tensor("l_mx", [128, 32], F32))
        den = ent(nc.sbuf_tensor("l_den", [128, 32], F32))
        xr = ent(nc.sbuf_tensor("l_xr", [128, 2 * 8 * 512], F32))
        rbf = ent(nc.sbuf_tensor("l_rbf", [128, 2 * 512], BF16))
        rsq = ent(nc.sbuf_tensor("l_rsq", [128, 2 * 512], BF16))
        stat = ent(nc.sbuf_tensor("l_stat", [1, 2 * 512], F32))
        statb = ent(nc.sbuf_tensor("l_statb", [1, 2 * 512], BF16))
        osb = ent(nc.sbuf_tensor("l_osb", [128, 2 * 512], F32))

        qtok = lambda c: arena[:, c * 1024:(c + 1) * 1024]
        ktok = lambda c: arena[:, 16384 + c * 1024: 16384 + (c + 1) * 1024]
        vsb = lambda oc: arena[:, oc * 2048:(oc + 1) * 2048]
        zm = lambda zc: arena[:, 16384 + zc * 2048: 16384 + (zc + 1) * 2048]

        P.wait("tensor", "dve", P.now("dve"))
        P.wait("tensor", "act", P.now("act"))
        P.wait("sync", "g", P.now("g"))
        P.op("sync", lambda e, o=wbuf[:, 0:8192], i=W["wqT"]:
             e.dma_start(out=o, in_=i), "ld", 16)
        P.op("sync", lambda e, o=wbuf[:, 8192:16384], i=W["wkT"]:
             e.dma_start(out=o, in_=i), "ld", 16)
        ld0 = P.now("ld")
        P.wait("gpsimd", "pe", P.now("pe"))
        for k in range(8):
            P.op("gpsimd", lambda e, o=mt(k)[:, EXT_OFF:EXT_OFF + T],
                 i=x_src[k * 128:(k + 1) * 128, :]:
                 e.dma_start(out=o, in_=i), "g", 16)
        gmir = P.now("g")
        P.wait("tensor", "g", gmir)
        P.wait("tensor", "ld", ld0)

        bankF = {}
        def bwait(eng, bk):
            if bankF.get(bk):
                P.wait(eng, bankF[bk][0], bankF[bk][1])
        for c in range(N_TCH):
            for qk, base, dst in ((0, 0, qtok(c)), (1, 8192, ktok(c))):
                for half in range(2):
                    bk = qk * 2 + half
                    bwait("tensor", bk)
                    for k in range(8):
                        P.op("tensor", lambda e, p=pb[bk][:, 0:512],
                             l=mt(k)[:, EXT_OFF + c * 128: EXT_OFF + c * 128 + 128],
                             r=wbuf[:, base + k * 1024 + half * 512:
                                    base + k * 1024 + half * 512 + 512],
                             s=(k == 0), st=(k == 7):
                             e.matmul(p, l, r, start=s, stop=st),
                             "pe" if k == 7 else None, 1)
                    pev = P.now("pe")
                    P.wait("scalar", "pe", pev)
                    aE = P.op("scalar", lambda e,
                              o=dst[:, half * 512: half * 512 + 512],
                              i=pb[bk][:, 0:512]:
                              e.activation(o, i, AF.Identity), "act", 1)
                    bankF[bk] = ("act", aE)
        act_qk = P.now("act")
        pe_qk = P.now("pe")

        P.wait("tensor", "act", act_qk)
        for hp in range(8):
            bk = 4 + hp // 4
            col = (hp % 4) * 128
            for c in range(N_TCH):
                P.op("tensor", lambda e, p=pb[bk][:, col:col + 128],
                     l=ktok(c)[:, hp * 128: hp * 128 + 128],
                     r=qtok(c)[:, hp * 128: hp * 128 + 128],
                     s=(c == 0), st=(c == N_TCH - 1):
                     e.matmul(p, l, r, start=s, stop=st),
                     "pe" if c == N_TCH - 1 else None, 1)
        pez1 = P.now("pe")
        P.wait("vector", "pe", pez1)
        P.op("vector", lambda e, o=z1f[:, 0:512], i=pb[4][:, 0:512]:
             e.tensor_copy(o, i), None)
        P.op("vector", lambda e, o=z1f[:, 512:1024], i=pb[5][:, 0:512]:
             e.tensor_copy(o, i), "dve", 1)
        dz1 = P.now("dve")
        P.wait("gpsimd", "dve", dz1)
        P.op("gpsimd", lambda e, o=z1cc_in[:], i=z1f[:]:
             e.dma_start(out=o, in_=i), "g", 16)
        P.op("gpsimd", lambda e: e.collective_compute(
            "AllReduce", ALU.add,
            replica_groups=[[0, 1, 2, 3], [4, 5, 6, 7]],
            ins=[z1cc_in.opt()], outs=[z1cc_out.opt()]), "pool", 1)
        ccn = P.now("pool")
        P.wait("gpsimd", "pool", ccn)
        P.op("gpsimd", lambda e, o=z1f[:], i=z1cc_out[:]:
             e.dma_start(out=o, in_=i), "g", 16)
        gz1 = P.now("g")

        P.wait("sync", "pe", pe_qk)
        P.op("sync", lambda e, o=wbuf[:, 0:8192], i=W["wvp"]:
             e.dma_start(out=o, in_=i), "ld", 16)
        P.op("sync", lambda e, o=wbuf[:, 8192:16384], i=W["wop"]:
             e.dma_start(out=o, in_=i), "ld", 16)
        ldv = P.now("ld")
        P.wait("tensor", "ld", ldv)
        for oc in range(8):
            for tc2 in range(4):
                bk = (oc * 4 + tc2) % 4
                bwait("tensor", bk)
                for k in range(8):
                    P.op("tensor", lambda e, p=pb[bk][:, 0:512],
                         l=wbuf[:, (oc * 8 + k) * 128:(oc * 8 + k) * 128 + 128],
                         r=mt(k)[:, EXT_OFF + tc2 * 512: EXT_OFF + tc2 * 512 + 512],
                         s=(k == 0), st=(k == 7):
                         e.matmul(p, l, r, start=s, stop=st),
                         "pe" if k == 7 else None, 1)
                pev = P.now("pe")
                P.wait("scalar", "pe", pev)
                aE = P.op("scalar", lambda e,
                          o=vsb(oc)[:, tc2 * 512: tc2 * 512 + 512],
                          i=pb[bk][:, 0:512]:
                          e.activation(o, i, AF.Identity), "act", 1)
                bankF[bk] = ("act", aE)

        P.wait("vector", "g", gz1)
        P.wait("scalar", "g", gz1)
        P.op("vector", lambda e, o=den[:, 0:8]: e.memset(o, 0.0), None)
        P.op("vector", lambda e, o=z2b[:]: e.memset(o, 0.0), "dve", 1)
        for h in range(16):
            hp, par = h // 2, h % 2
            p0, p1 = par * 64, par * 64 + 64
            cb = hp * 128 + par * 64
            sl = z1f[p0:p1, cb:cb + 64]
            P.op("vector", lambda e, o=mx[p0:p1, h:h + 1], i=sl:
                 e.reduce_max(o, i, axis=AX.X), None)
            P.op("vector", lambda e, o=mx[p0:p1, 16 + h:17 + h],
                 i=mx[p0:p1, h:h + 1], s_=fcv(C_NEIGHTH, p0, p1):
                 e.tensor_scalar_mul(o, i, s_), "dve", 1)
            dm = P.now("dve")
            P.wait("scalar", "dve", dm)
            P.op("scalar", lambda e, o=z2f[p0:p1, cb:cb + 64], i=sl,
                 sc_=fcv(C_EIGHTH, p0, p1), b=mx[p0:p1, 16 + h:17 + h],
                 acc=den[p0:p1, hp:hp + 1]:
                 e.activation(o, i, AF.Exp, scale=sc_, bias=b, accum_out=acc),
                 "act", 1)
        aexp = P.now("act")
        P.wait("scalar", "act", aexp)
        P.op("scalar", lambda e, o=den[:, 16:24], i=den[:, 0:8]:
             e.activation(o, i, AF.Ln), None)
        P.op("scalar", lambda e, o=den[:, 16:24], i=den[:, 16:24],
             sc_=fcv(C_NONE):
             e.activation(o, i, AF.Exp, scale=sc_), "act", 1)
        azr = P.now("act")
        P.wait("vector", "act", azr)
        for h in range(16):
            hp, par = h // 2, h % 2
            p0, p1 = par * 64, par * 64 + 64
            cb = hp * 128 + par * 64
            P.op("vector", lambda e, o=z2b[p0:p1, cb:cb + 64],
                 i=z2f[p0:p1, cb:cb + 64], s_=den[p0:p1, 16 + hp:17 + hp]:
                 e.tensor_scalar_mul(o, i, s_), "dve", 1)
        dz2 = P.now("dve")

        P.wait("tensor", "dve", dz2)
        for hp in range(8):
            for tc2 in range(4):
                bk = 4 + (hp * 4 + tc2) % 2
                bwait("tensor", bk)
                P.op("tensor", lambda e, p=pb[bk][:, 0:512],
                     l=z2b[:, hp * 128: hp * 128 + 128],
                     r=vsb(hp)[:, tc2 * 512: tc2 * 512 + 512]:
                     e.matmul(p, l, r, start=True, stop=True), "pe", 1)
                pev = P.now("pe")
                P.wait("scalar", "pe", pev)
                aE = P.op("scalar", lambda e,
                          o=zm(hp)[:, tc2 * 512: tc2 * 512 + 512],
                          i=pb[bk][:, 0:512]:
                          e.activation(o, i, AF.Identity), "act", 1)
                bankF[bk] = ("act", aE)
        act_zm = P.now("act")
        dve_zm = P.now("dve")

        sc_ln = {"rbf": rbf, "rsq": rsq, "stat": stat, "statb": statb,
                 "osb": osb}
        track = {"store_dves": [], "gstores": [], "dver": 0}
        P.wait("tensor", "act", act_zm)
        P.wait("tensor", "dve", dve_zm)
        dver_g = {}
        apply_dve = {}
        for g in range(5):
            if g < 4:
                sl = (g % 2) * 4096
                P.wait("sync", "dve", apply_dve.get(g - 2, 0))
                for fc_ in range(8):
                    P.op("sync", lambda e, o=xr[:, sl + fc_ * 512: sl + (fc_ + 1) * 512],
                         i=x_src[fc_ * 128:(fc_ + 1) * 128, g * 512: g * 512 + 512]:
                         e.dma_start(out=o, in_=i), "ld", 16)
                ldx = P.now("ld")
                for o_ in range(8):
                    bk = (o_ % 2)
                    bwait("tensor", bk)
                    for zc in range(8):
                        P.op("tensor", lambda e, p=pb[bk][:, 0:512],
                             l=wbuf[:, 8192 + (o_ * 8 + zc) * 128:
                                    8192 + (o_ * 8 + zc) * 128 + 128],
                             r=zm(zc)[:, g * 512: g * 512 + 512],
                             s=(zc == 0), st=(zc == 7):
                             e.matmul(p, l, r, start=s, stop=st),
                             "pe" if zc == 7 else None, 1)
                    pev = P.now("pe")
                    P.wait("vector", "pe", pev)
                    if o_ == 0:
                        P.wait("vector", "ld", ldx)
                    dW = P.op("vector", lambda e,
                              o=xr[:, sl + o_ * 512: sl + (o_ + 1) * 512],
                              i0=pb[bk][:, 0:512],
                              i1=xr[:, sl + o_ * 512: sl + (o_ + 1) * 512]:
                              e.tensor_add(o, i0, i1), "dve", 1)
                    bankF[bk] = ("dve", dW)
                dver_g[g] = P.now("dve")
            if g >= 1:
                gg = g - 1
                track["dver"] = dver_g[gg]
                emit_ln_group(P, R, sc_ln, xr[:, (gg % 2) * 4096:(gg % 2) * 4096 + 4096],
                              x_dst, gg, track)
                apply_dve[gg] = track["dver"]

        P.flush(block)


def pack_l1_attn(inp):
    d = {}
    for nm, key in (("wqT", "L1_attn_Wq"), ("wkT", "L1_attn_Wk")):
        w = bf(np.asarray(inp[key], np.float32).T).reshape(8, 128, 1024)
        d[nm] = np.ascontiguousarray(w.transpose(1, 0, 2).reshape(128, 8192))
    d["wvp"] = pack_lhsT(bf(np.asarray(inp["L1_attn_Wv"], np.float32).T), 1024, 1024)
    d["wop"] = pack_lhsT(bf(np.asarray(inp["L1_attn_Wo"], np.float32).T), 1024, 1024)
    return d


L1A_DRAM = [
    ("wqT", [128, 8192], BF16), ("wkT", [128, 8192], BF16),
    ("wvp", [128, 8192], BF16), ("wop", [128, 8192], BF16),
]





def plan_l1_ffn(nc, P, block, R, x_src, x_dst):
    pb = R["pb"]
    mir = R["mir"]
    mt = lambda k: mir[:, k * TE:(k + 1) * TE]
    W = R["W"]
    TC = 512

    with ExitStack() as es:
        ent = es.enter_context
        bd1 = ent(nc.sbuf_tensor("n_bd1", [128, 2 * 128], BF16))
        bd2 = ent(nc.sbuf_tensor("n_bd2", [128, 2 * 512], BF16))
        hsb = ent(nc.sbuf_tensor("n_hsb", [128, 128 * TC], BF16))
        ysb = ent(nc.sbuf_tensor("n_ysb", [128, 4 * 512], BF16))
        wcp = ent(nc.sbuf_tensor("n_wcp", [128, 3 * 2048], BF16))
        xr = ent(nc.sbuf_tensor("n_xr", [128, 8 * 512], F32))
        rbf = ysb[:, 0:1024]
        rsq = ysb[:, 1024:2048]
        stat = ent(nc.sbuf_tensor("n_stat", [1, 2 * 512], F32))
        statb = ent(nc.sbuf_tensor("n_statb", [1, 2 * 512], BF16))
        osb = ent(nc.sbuf_tensor("n_osb", [128, 2 * 512], F32))

        P.wait("tensor", "dve", P.now("dve"))
        P.wait("tensor", "act", P.now("act"))
        P.wait("sync", "g", P.now("g"))
        ld0 = P.now("ld")
        P.wait("gpsimd", "pe", P.now("pe"))
        for k in range(8):
            P.op("gpsimd", lambda e, o=mt(k)[:, EXT_OFF:EXT_OFF + T],
                 i=x_src[k * 128:(k + 1) * 128, :]:
                 e.dma_start(out=o, in_=i), "g", 16)
        gmir = P.now("g")
        P.wait("tensor", "g", gmir)
        P.wait("tensor", "ld", ld0)

        sc_ln = {"rbf": rbf, "rsq": rsq, "stat": stat, "statb": statb,
                 "osb": osb}
        track = {"store_dves": [], "gstores": [], "dver": 0}
        bankF = {}
        def bwait(eng, bk):
            if bankF.get(bk):
                P.wait(eng, bankF[bk][0], bankF[bk][1])
        bd2_pe = {}
        wcp_pe = {}

        for t in range(4):
            c0 = EXT_OFF + t * TC
            P.wait("tensor", "dve", P.now("dve"))
            P.wait("tensor", "act", P.now("act"))
            for hg in range(32):
                sl4 = hg % 2
                if bd2_pe.get(sl4):
                    P.wait("sync", "pe", bd2_pe[sl4])
                P.op("sync", lambda e, o=bd1[:, sl4 * 128:(sl4 + 1) * 128],
                     i=W["bd1"][:, hg * 128:(hg + 1) * 128]:
                     e.dma_start(out=o, in_=i), "ld", 16)
                P.op("sync", lambda e, o=bd2[:, sl4 * 512:(sl4 + 1) * 512],
                     i=W["bd2"][:, hg * 512:(hg + 1) * 512]:
                     e.dma_start(out=o, in_=i), "ld", 16)
                ldb = P.now("ld")
                g8 = hg % 8
                yb = hg % 2
                bwait("tensor", yb)
                P.wait("tensor", "ld", ldb)
                P.op("tensor", lambda e, p=pb[yb][:, 0:TC],
                     l=bd1[:, sl4 * 128: sl4 * 128 + 128],
                     r=mt(g8)[:, c0:c0 + TC]:
                     e.matmul(p, l, r, start=True, stop=True), "pe", 1)
                pey = P.now("pe")
                P.wait("vector", "pe", pey)
                ysl = (hg % 2) * TC
                dY = P.op("vector", lambda e, o=ysb[:, ysl:ysl + TC],
                          i=pb[yb][:, 0:TC]:
                          e.tensor_copy(o, i), "dve", 1)
                bankF[yb] = ("dve", dY)
                P.wait("tensor", "ld", ldb)
                P.wait("tensor", "dve", dY)
                for qq in range(2):
                    zb = ("z", (hg * 2 + qq) % 2)
                    pz = R["pbig"][1] if zb[1] == 0 else R["pbig"][3]
                    bwait("tensor", zb)
                    for q in (2 * qq, 2 * qq + 1):
                        kc = hg * 4 + q
                        P.op("tensor", lambda e, p=pz[:, (q % 2) * TC:(q % 2) * TC + TC],
                             l=bd2[:, sl4 * 512 + q * 128: sl4 * 512 + q * 128 + 128],
                             r=ysb[:, ysl:ysl + TC]:
                             e.matmul(p, l, r, start=True, stop=True),
                             "pe" if q % 2 == 1 else None, 1)
                    pez = P.now("pe")
                    bd2_pe[sl4] = pez
                    P.wait("scalar", "pe", pez)
                    kc0 = hg * 4 + 2 * qq
                    aZ = P.op("scalar", lambda e,
                              o=hsb[:, kc0 * TC:(kc0 + 2) * TC],
                              i=pz[:, 0:2 * TC]:
                              e.activation(o, i, AF.Gelu), "act", 1)
                    bankF[zb] = ("act", aZ)
                if hg == 15:
                    act_half0 = P.now("act")
            act_h = P.now("act")
            P.wait("sync", "dve", track["dver"])
            for fc_ in range(8):
                P.op("sync", lambda e, o=xr[:, fc_ * 512:(fc_ + 1) * 512],
                     i=x_src[fc_ * 128:(fc_ + 1) * 128, t * 512: t * 512 + 512]:
                     e.dma_start(out=o, in_=i), "ld", 16)
            ldx = P.now("ld")
            P.wait("tensor", "act", act_half0)
            if t == 0:
                for bk in (4, 5):
                    for k32 in range(32):
                        P.op("tensor", lambda e, p=pb[bk][:, 0:512],
                             l=hsb[:, 0:128], r=hsb[:, 0:512],
                             s=(k32 == 0), st=(k32 == 31):
                             e.matmul(p, l, r, start=s, stop=st),
                             "pe" if k32 == 31 else None, 1)
            ldx_waited = [False]
            slabctr = [0]
            for q2 in range(2):
                P.wait("tensor", "act", act_half0 if q2 == 0 else act_h)
                for o_ in range(8):
                  for q4 in (2 * q2, 2 * q2 + 1):
                    ob = 4 + (o_ * 4 + q4) % 2
                    bwait("tensor", ob)
                    for e8 in range(2):
                        slab = q4 * 2 + e8
                        sl3 = slabctr[0] % 3
                        slabctr[0] += 1
                        if wcp_pe.get(sl3):
                            P.wait("sync", "pe", wcp_pe[sl3])
                        P.op("sync", lambda e, o=wcp[:, sl3 * 2048:(sl3 + 1) * 2048],
                             i=W["wcp"][:, o_ * 16384 + slab * 2048:
                                        o_ * 16384 + slab * 2048 + 2048]:
                             e.dma_start(out=o, in_=i), "ld", 16)
                        ldw = P.now("ld")
                        P.wait("tensor", "ld", ldw)
                        for k16 in range(16):
                            kk = e8 * 16 + k16
                            kc = q4 * 32 + kk
                            P.op("tensor", lambda e, p=pb[ob][:, 0:TC],
                                 l=wcp[:, sl3 * 2048 + k16 * 128:
                                        sl3 * 2048 + k16 * 128 + 128],
                                 r=hsb[:, kc * TC:(kc + 1) * TC],
                                 s=(kk == 0), st=(kk == 31):
                                 e.matmul(p, l, r, start=s, stop=st),
                                 "pe" if k16 == 15 else None, 1)
                        wcp_pe[sl3] = P.now("pe")
                    pev = P.now("pe")
                    P.wait("vector", "pe", pev)
                    if not ldx_waited[0]:
                        P.wait("vector", "ld", ldx)
                        ldx_waited[0] = True
                    dW = P.op("vector", lambda e,
                              o=xr[:, o_ * 512:(o_ + 1) * 512],
                              i0=pb[ob][:, 0:TC],
                              i1=xr[:, o_ * 512:(o_ + 1) * 512]:
                              e.tensor_add(o, i0, i1), "dve", 1)
                    bankF[ob] = ("dve", dW)
            track["dver"] = P.now("dve")
            emit_ln_group(P, R, sc_ln, xr, x_dst, t, track)

        P.flush(block)


def pack_l1_ffn(inp):
    d = {}
    rm = np.asarray(inp["L1_ffn_rail_maker"], np.float32)
    rup = np.asarray(inp["L1_ffn_rail_up_proj"], np.float32)
    bd1 = np.zeros((128, 32 * 128), np.float32)
    bd2 = np.zeros((128, 32 * 512), np.float32)
    for h in range(4):
        for g8 in range(8):
            for r in range(8):
                rg = 8 * g8 + r
                bd1[r * 16:(r + 1) * 16,
                    (h * 8 + g8) * 128 + r * 16:(h * 8 + g8) * 128 + r * 16 + 16] = rm[h, rg]
                bd2[r * 16:(r + 1) * 16,
                    (h * 8 + g8) * 512 + r * 64:(h * 8 + g8) * 512 + r * 64 + 64] = rup[h, rg].T
    d["bd1"] = bf(bd1)
    d["bd2"] = bf(bd2)
    d["wcp"] = pack_lhsT(bf(np.asarray(inp["L1_ffn_w_cproj"], np.float32).T),
                         16384, 1024, oc_outer=True)
    return d


L1F_DRAM = [
    ("bd1", [128, 32 * 128], BF16), ("bd2", [128, 32 * 512], BF16),
    ("wcp", [128, 128 * 1024], BF16),
]



def declare_common(nc):
    es = ExitStack()
    ent = es.enter_context
    R = {}
    sems = {}
    for name in ("ld", "g", "pe", "act", "dve", "pool"):
        sems[name] = ent(nc.semaphore(name))
    R["sems"] = sems
    pbig = []
    for i in range(4):
        pbig.append(ent(nc.psum_tensor(f"pbig{i}", [128, 1024], F32)))
    R["pbig"] = [p[:] for p in pbig]
    R["pb"] = [pbig[i // 2][:, (i % 2) * 512:(i % 2) * 512 + 512] for i in range(8)]
    R["pbf"] = pbig[3][:, 512:1024].bitcast(BF16)
    mir = ent(nc.sbuf_tensor("mir", [128, 8 * TE], BF16))
    R["mir"] = mir[:]
    ident = ent(nc.sbuf_tensor("c_ident", [128, 128], BF16))
    R["ident"] = ident[:]
    ones_row = ent(nc.sbuf_tensor("c_ones_row", [1, 128], BF16))
    R["ones_row"] = ones_row[:]
    st_a = ent(nc.sbuf_tensor("c_st_a", [128, 2], BF16))
    R["st_a"] = st_a[:]
    st_b = ent(nc.sbuf_tensor("c_st_b", [128, 2], BF16))
    R["st_b"] = st_b[:]
    fconst = ent(nc.sbuf_tensor("c_fconst", [128, 12], F32))
    R["fconst"] = fconst[:]
    return R, es


def load_common_consts(nc, P, R):
    W = R["W"]
    for dst, src in [(R["ident"], W["ident"]), (R["ones_row"], W["ones_row"]),
                     (R["st_a"], W["st_a"]), (R["st_b"], W["st_b"]),
                     (R["fconst"], W["fconst"])]:
        P.op("sync", lambda e, o=dst, i=src: e.dma_start(out=o, in_=i), "ld", 16)


def common_inputs():
    d = {}
    d["ident"] = bf(np.eye(128, dtype=np.float32))
    d["ones_row"] = bf(np.ones((1, 128), np.float32))
    sa = np.zeros((128, 2), np.float32); sa[:, 0] = 1.0
    sb = np.zeros((128, 2), np.float32); sb[:, 1] = 1.0
    d["st_a"] = bf(sa)
    d["st_b"] = bf(sb)
    fc = np.array([1e-8, 1e-5, 1.0 / 1024.0, 0.0, 1.0, -0.5, -1.0, 0.125,
                   -0.125, 0.0, 0.0, 0.0], np.float32)
    d["fconst"] = np.ascontiguousarray(np.tile(fc, (128, 1)))
    return d


def pack_l0_attn(inp):
    d = {}
    wqkv = np.asarray(inp["L0_attn_w_qkv"], np.float32)
    qk = wqkv[:, 0:2048]
    d["wqkv_qk"] = pack_lhsT(bf(qk), 1024, 2048, oc_outer=True)
    v = bf(wqkv[:, 2048:3072]).reshape(8, 128, 1024)
    d["wqkv_v"] = np.ascontiguousarray(v.transpose(1, 0, 2).reshape(128, 8 * 1024))
    d["proj"] = bf(np.asarray(inp["L0_attn_proj"], np.float32))
    d["wout"] = pack_lhsT(bf(np.asarray(inp["L0_attn_w_out"], np.float32)), 1024, 1024)
    d["b_out"] = np.ascontiguousarray(
        np.asarray(inp["L0_attn_b_out"], np.float32).reshape(8, 128).T)
    return d



def _finalize_outputs(P, block):
    P.wait("gpsimd", "g", P.now("g"))
    P.flush(block)


def build_l0_attn_program():
    nc = bass.Bass("TRN2", target_bir_lowering=False)
    x_src = nc.dram_tensor("xT", [D, TE], F32, kind="ExternalInput").ap()
    x_dst = nc.dram_tensor("x1", [D, T], F32, kind="ExternalOutput").ap()
    W = {}
    for name, shape, dt in L0A_DRAM + COMMON_DRAM:
        W[name] = nc.dram_tensor(name, shape, dt, kind="ExternalInput").ap()
    R, es = declare_common(nc)
    R["W"] = W
    with es:
        with nc.Block() as block:
            P = Prog(R["sems"])
            load_common_consts(nc, P, R)
            plan_l0_attn(nc, P, block, R, x_src, x_dst, es)
            _finalize_outputs(P, block)
    return nc


COMMON_DRAM = [
    ("ident", [128, 128], BF16), ("ones_row", [1, 128], BF16),
    ("st_a", [128, 2], BF16), ("st_b", [128, 2], BF16),
    ("fconst", [128, 12], F32),
]


def build_l0_ffn_program():
    nc = bass.Bass("TRN2", target_bir_lowering=False)
    x_src = nc.dram_tensor("x1", [D, T], F32, kind="ExternalInput").ap()
    x_dst = nc.dram_tensor("x2", [D, T], F32, kind="ExternalOutput").ap()
    W = {}
    for name, shape, dt in L0F_DRAM + COMMON_DRAM:
        W[name] = nc.dram_tensor(name, shape, dt, kind="ExternalInput").ap()
    R, es = declare_common(nc)
    R["W"] = W
    with es:
        with nc.Block() as block:
            P = Prog(R["sems"])
            load_common_consts(nc, P, R)
            plan_l0_ffn(nc, P, block, R, x_src, x_dst)
            _finalize_outputs(P, block)
    return nc


def build_l1_attn_program():
    nc = bass.Bass("TRN2", target_bir_lowering=False)
    x_src = nc.dram_tensor("x2", [D, T], F32, kind="ExternalInput").ap()
    x_dst = nc.dram_tensor("x3", [D, T], F32, kind="ExternalOutput").ap()
    z1cc_in = nc.dram_tensor("z1cc_in", [128, 1024], F32, kind="Internal").ap()
    z1cc_out = nc.dram_tensor("z1cc_out", [128, 1024], F32, kind="Internal").ap()
    W = {}
    for name, shape, dt in L1A_DRAM + COMMON_DRAM:
        W[name] = nc.dram_tensor(name, shape, dt, kind="ExternalInput").ap()
    R, es = declare_common(nc)
    R["W"] = W
    with es:
        with nc.Block() as block:
            P = Prog(R["sems"])
            load_common_consts(nc, P, R)
            plan_l1_attn(nc, P, block, R, x_src, x_dst, z1cc_in, z1cc_out)
            _finalize_outputs(P, block)
    return nc


def build_l1_ffn_program():
    nc = bass.Bass("TRN2", target_bir_lowering=False)
    x_src = nc.dram_tensor("x3", [D, T], F32, kind="ExternalInput").ap()
    x_dst = nc.dram_tensor("x4", [D, T], F32, kind="ExternalOutput").ap()
    W = {}
    for name, shape, dt in L1F_DRAM + COMMON_DRAM:
        W[name] = nc.dram_tensor(name, shape, dt, kind="ExternalInput").ap()
    R, es = declare_common(nc)
    R["W"] = W
    with es:
        with nc.Block() as block:
            P = Prog(R["sems"])
            load_common_consts(nc, P, R)
            plan_l1_ffn(nc, P, block, R, x_src, x_dst)
            _finalize_outputs(P, block)
    return nc


L0A_DRAM = [
    ("wqkv_qk", [128, 16 * 8 * 128], BF16), ("wqkv_v", [128, 8 * 1024], BF16),
    ("proj", [128, 64], BF16), ("wout", [128, 64 * 128], BF16),
    ("b_out", [128, 8], F32),
    ("mask_first", [128, 128], BF16), ("mask_mid", [128, 128], BF16),
    ("mask_ext_mid", [8, 128], BF16), ("mask_ext_last", [8, 128], BF16),
]


def build_fused_program():
    nc = bass.Bass("TRN2", target_bir_lowering=False)
    xT = nc.dram_tensor("xT", [D, TE], F32, kind="ExternalInput").ap()
    out = nc.dram_tensor("outT", [D, T], F32, kind="ExternalOutput").ap()
    x1 = nc.dram_tensor("x1s", [D, T], F32, kind="Internal").ap()
    x2 = nc.dram_tensor("x2s", [D, T], F32, kind="Internal").ap()
    x3 = nc.dram_tensor("x3s", [D, T], F32, kind="Internal").ap()
    z1cc_in = nc.dram_tensor("z1cc_in", [128, 1024], F32, kind="Internal").ap()
    z1cc_out = nc.dram_tensor("z1cc_out", [128, 1024], F32, kind="Internal").ap()
    W = {}
    for name, shape, dt in L0A_DRAM + L0F_DRAM + L1A_DRAM + L1F_DRAM + COMMON_DRAM:
        W[name] = nc.dram_tensor(name, shape, dt, kind="ExternalInput").ap()
    R, es = declare_common(nc)
    R["W"] = W
    with es:
        with nc.Block() as block:
            P = Prog(R["sems"])
            load_common_consts(nc, P, R)
            plan_l0_attn(nc, P, block, R, xT, x1, es)
            plan_l0_ffn(nc, P, block, R, x1, x2)
            plan_l1_attn(nc, P, block, R, x2, x3, z1cc_in, z1cc_out)
            plan_l1_ffn(nc, P, block, R, x3, out)
            _finalize_outputs(P, block)
    return nc


def kernel(**inputs):
    x = np.asarray(inputs["L0_attn_x"], np.float32)
    wall = {}
    wall.update(pack_l0_attn(inputs))
    wall.update(pack_l0_ffn(inputs))
    wall.update(pack_l1_attn(inputs))
    wall.update(pack_l1_ffn(inputs))
    wall.update(common_inputs())
    nc = build_fused_program()
    in_maps = []
    for c in range(NCORES):
        mf, mm, mem, mel = make_masks(c)
        m = {"xT": make_xT_ext(x, c),
             "mask_first": mf, "mask_mid": mm,
             "mask_ext_mid": mem, "mask_ext_last": mel}
        m.update(wall)
        in_maps.append(m)
    res = run_bass_kernel_spmd(nc, in_maps, core_ids=list(range(NCORES)))
    outs = [np.asarray(res.results[c]["outT"]).T for c in range(NCORES)]
    return np.stack(outs).reshape(2, 8192, 1024)


def kernel_unfused(**inputs):
    x = np.asarray(inputs["L0_attn_x"], np.float32)
    wl0a = pack_l0_attn(inputs)
    wl0f = pack_l0_ffn(inputs)
    com = common_inputs()

    nc = build_l0_attn_program()
    in_maps = []
    for c in range(NCORES):
        mf, mm, mem, mel = make_masks(c)
        m = {"xT": make_xT_ext(x, c),
             "mask_first": mf, "mask_mid": mm,
             "mask_ext_mid": mem, "mask_ext_last": mel}
        m.update(wl0a)
        m.update(com)
        in_maps.append(m)
    res = run_bass_kernel_spmd(nc, in_maps, core_ids=list(range(NCORES)))
    x1s = [np.asarray(res.results[c]["x1"]) for c in range(NCORES)]

    nc = build_l0_ffn_program()
    in_maps = []
    for c in range(NCORES):
        m = {"x1": x1s[c]}
        m.update(wl0f)
        m.update(com)
        in_maps.append(m)
    res = run_bass_kernel_spmd(nc, in_maps, core_ids=list(range(NCORES)))
    x2s = [np.asarray(res.results[c]["x2"]) for c in range(NCORES)]

    wl1a = pack_l1_attn(inputs)
    nc = build_l1_attn_program()
    in_maps = []
    for c in range(NCORES):
        m = {"x2": x2s[c]}
        m.update(wl1a)
        m.update(com)
        in_maps.append(m)
    res = run_bass_kernel_spmd(nc, in_maps, core_ids=list(range(NCORES)))
    x3s = [np.asarray(res.results[c]["x3"]) for c in range(NCORES)]

    wl1f = pack_l1_ffn(inputs)
    nc = build_l1_ffn_program()
    in_maps = []
    for c in range(NCORES):
        m = {"x3": x3s[c]}
        m.update(wl1f)
        m.update(com)
        in_maps.append(m)
    res = run_bass_kernel_spmd(nc, in_maps, core_ids=list(range(NCORES)))
    outs = [np.asarray(res.results[c]["x4"]).T for c in range(NCORES)]
    full = np.stack(outs).reshape(2, 8192, 1024)
    return full
